# version 08
# baseline (speedup 1.0000x reference)
import numpy as np
import concourse.bass as bass
import concourse.mybir as mybir
from concourse.bass import IndirectOffsetOnAxis
from concourse.bass_utils import run_bass_kernel_spmd

F32 = mybir.dt.float32
BF16 = mybir.dt.bfloat16
AF = mybir.ActivationFunctionType
ALU = mybir.AluOpType
AX = mybir.AxisListType.X

D = 1024
S = 2048
NB = 4
H = 16
DK = 64
DC = 256
E = 8
HID = 4096
NT = S // 128
EPS = 1e-5
NEG = -1.0e30
CAP = 768
BIGIX = 1.0e8
I32 = mybir.dt.int32

ENGSEM = {"tensor": "pe", "scalar": "ac", "vector": "dv", "gpsimd": "po",
          "sync": "spq"}


def _mm(o, l, r, st, sp):
    def f(e):
        return e.matmul(o, l, r, start=st, stop=sp)
    return f


def _tr(o, s, idn):
    def f(e):
        return e.transpose(o, s, idn)
    return f


def _cp(o, s):
    def f(e):
        return e.copy(out=o, in_=s)
    return f


def _mulc(o, s, c):
    def f(e):
        return e.mul(out=o, in_=s, mul=c)
    return f


def _act(o, s, func, bias=None, accum=None, scale=None):
    def f(e):
        kw = {}
        if bias is not None:
            kw["bias"] = bias
        if accum is not None:
            kw["accum_out"] = accum
        if scale is not None:
            kw["scale"] = scale
        return e.activation(out=o, in_=s, func=func, **kw)
    return f


def _dma(o, s, transpose=False):
    def f(e):
        if transpose:
            return e.dma_start(out=o, in_=s, transpose=True)
        return e.dma_start(out=o, in_=s)
    return f


def _memset(ap, v):
    def f(e):
        return e.memset(ap, v)
    return f


def _asel(o, cmp, fill, base, pattern, cm):
    def f(e):
        return e.affine_select(out=o, in_=o, compare_op=cmp, fill=fill,
                               base=base, pattern=pattern,
                               channel_multiplier=cm)
    return f


def _rsum(o, s):
    def f(e):
        return e.reduce_sum(o, s, axis=AX)
    return f


def _rmaxn(o, s):
    def f(e):
        return e.reduce_max(o, s, axis=AX, negate=True)
    return f


def _ts(o, i0, s1, s2, op0, op1=None):
    def f(e):
        if op1 is None:
            return e.tensor_scalar(out=o, in0=i0, scalar1=s1, scalar2=None,
                                   op0=op0)
        return e.tensor_scalar(out=o, in0=i0, scalar1=s1, scalar2=s2,
                               op0=op0, op1=op1)
    return f


def _tt(o, i0, i1, op):
    def f(e):
        return e.tensor_tensor(out=o, in0=i0, in1=i1, op=op)
    return f


def _rcpf(o, s):
    def f(e):
        if o.dtype == BF16:
            with e.bass.allow_low_precision(
                    reason="attn denominator reciprocal cast to bf16 for the "
                           "broadcast matmul; ~0.4% scale noise, well inside "
                           "tolerance"):
                return e.reciprocal(o, s)
        return e.reciprocal(o, s)
    return f


def _tcopy(o, s):
    def f(e):
        return e.tensor_copy(out=o, in_=s)
    return f


def _igather(o, src, idx_ap, bound, eoff=0):
    def f(e):
        return e.indirect_dma_start(
            out=o, out_offset=None, in_=src,
            in_offset=IndirectOffsetOnAxis(idx_ap, 0),
            element_offset=eoff, bounds_check=bound, oob_is_err=False)
    return f


def _iscatter(o, idx_ap, src, bound, op=None):
    def f(e):
        return e.indirect_dma_start(
            out=o, out_offset=IndirectOffsetOnAxis(idx_ap, 0), in_=src,
            in_offset=None, bounds_check=bound, oob_is_err=False,
            **({"compute_op": op} if op is not None else {}))
    return f


def _iota(o, pattern, base, cm):
    def f(e):
        return e.iota(o, pattern, base=base, channel_multiplier=cm)
    return f


class Res:
    __slots__ = ("w", "r", "name")

    def __init__(self, name=""):
        self.w = None
        self.r = []
        self.name = name


class Prog:
    def __init__(self, nc):
        self.nc = nc
        self.ops = []
        self.C = {}

    def _bump(self, sem, k):
        self.C[sem] = self.C.get(sem, 0) + k
        return (sem, self.C[sem])

    def op(self, engine, fn, reads=(), writes=(), sem=None, dma=False,
           group=None, last=True, extra=()):
        waits = {}

        def merge(evt):
            if evt is not None:
                s, v = evt
                if waits.get(s, -1) < v:
                    waits[s] = v

        rr = list(reads)
        ww = list(writes)
        for r in rr:
            merge(r.w)
        for w in ww:
            merge(w.w)
            for ev in w.r:
                merge(ev)
        for ev in extra:
            merge(ev)

        semn = sem or ENGSEM[engine]
        k = 16 if dma else 1
        evt = None
        if last:
            evt = self._bump(semn, k)
        self.ops.append((engine, fn, waits, (semn, k) if last else None))

        if group is not None:
            group.extend(rr)
            if last:
                for r in group:
                    r.r.append(evt)
                for w in ww:
                    w.w = evt
                    w.r = []
                del group[:]
        else:
            assert last
            for r in rr:
                r.r.append(evt)
            for w in ww:
                w.w = evt
                w.r = []
        return evt

    def barrier_wait(self, engine, evts):
        waits = {}
        for s, v in evts:
            if waits.get(s, -1) < v:
                waits[s] = v
        self.ops.append((engine, None, waits, None))

    def now(self, sem):
        return (sem, self.C.get(sem, 0))

    def replay(self, block, sems):
        done = {}

        def run(eng_name):
            def body(e):
                for (en, fn, waits, inc) in self.ops:
                    if en != eng_name:
                        continue
                    for s, v in sorted(waits.items()):
                        if done.get((eng_name, s), -1) >= v:
                            continue
                        e.wait_ge(sems[s], v)
                        done[(eng_name, s)] = v
                    if fn is None:
                        continue
                    inst = fn(e)
                    if inc is not None:
                        s, k = inc
                        inst.then_inc(sems[s], k)
            return body

        block.sync(run("sync"))
        block.scalar(run("scalar"))
        block.vector(run("vector"))
        block.tensor(run("tensor"))
        block.gpsimd(run("gpsimd"))


class Bufs:
    pass


def alloc_bufs(nc, ctx):
    B = Bufs()

    def sb(name, shape, dt):
        return ctx.enter_context(nc.sbuf_tensor(name, shape, dt))

    B.XBUF = sb("XBUF", [128, NT * D], F32)
    B.ABUF = sb("ABUF", [128, 8 * S], BF16)
    B.WB1 = sb("WB1", [128, 16384], BF16)
    B.WB2 = sb("WB2", [128, 16 * 1040], BF16)
    B.HB = sb("HB", [128, 8192], BF16)
    B.MB = sb("MB", [128, 8192], BF16)
    B.SMALL = sb("SMALL", [128, 480], F32)
    B.SCR = sb("SCR", [128, 1024], F32)
    B.IDN = sb("IDN", [128, 128], BF16)
    B.MSK = sb("MSK", [128, 2048], BF16)
    B.EPSC = sb("EPSC", [128, 1], F32)
    B.INVD = sb("INVD", [128, 1], F32)
    B.NEGH = sb("NEGH", [128, 1], F32)
    B.ONES = sb("ONES", [1, 128], BF16)
    B.SUB = sb("SUB", [128, 128], BF16)
    B.ONESF = sb("ONESF", [128, 1], F32)
    B.ONESR = sb("ONESR", [1, 128], F32)
    B.IOTA = sb("IOTA", [128, 16], I32)
    B.IOTAF = sb("IOTAF", [128, 16], F32)
    B.IDXW = sb("IDXW", [128, 8], I32)
    B.IDXS = sb("IDXS", [128, 48], I32)
    B.GG = sb("GG", [128, 48], F32)
    B.CNTB = sb("CNTB", [128, 8], F32)
    B.SMSK = sb("SMSK", [128, 48], F32)
    B.RCP = sb("RCP", [1, 512], BF16)
    B.B2SB = sb("B2SB", [8, 1024], BF16)
    B.RBSB = sb("RBSB", [1, 8], BF16)
    B.PS = ctx.enter_context(nc.psum_tensor("PS", [128, 4096], F32))

    B.r_x = [Res(f"x{t}") for t in range(NT)]
    B.r_xnT = [Res(f"xnT{c}") for c in range(8)]
    B.r_xnTh = [Res(f"xnTh{c}") for c in range(8)]
    B.r_wb1 = Res("wb1")
    B.r_wb2 = Res("wb2")
    B.r_hb = [Res(f"hb{j}") for j in range(4)]
    B.r_mb = [Res(f"mb{j}") for j in range(16)]
    B.r_msk = Res("msk")
    B.r_stat = [Res("stat0"), Res("stat1")]
    B.r_aux = [Res(f"aux{j}") for j in range(8)]
    B.r_scr = Res("scr")
    B.r_idn = Res("idn")
    B.r_eps = Res("eps")
    B.r_sub = Res("sub")
    B.r_onesf = Res("onesf")
    B.r_iota = Res("iota")
    B.r_idxw = Res("idxw")
    B.r_idxs = Res("idxs")
    B.r_gg = Res("gg")
    B.r_cntb = Res("cntb")
    B.r_smsk = Res("smsk")
    B.r_xout = Res("xout")
    B.r_xnd = Res("xnd")
    B.r_gd = Res("gd")
    B.r_idxd = Res("idxd")
    B.r_ones = Res("ones")
    B.r_rcp = Res("rcp")
    B.r_b2 = Res("b2")
    B.r_rb = Res("rb")
    B.r_bank = [Res(f"ps{b}") for b in range(8)]
    return B


def bank(B, i, n=1, part=128, dt=F32):
    ap = B.PS[0:part, i * 512:(i + n) * 512]
    return ap.bitcast(BF16) if dt is BF16 else ap


def xtile(B, t):
    return B.XBUF[:, t * D:(t + 1) * D]


def xnT(B, c):
    return B.ABUF[:, c * S:(c + 1) * S]


def xnT_res(B, c, lo, hi):
    r = []
    if lo < 1024:
        r.append(B.r_xnTh[c])
    if hi > 1024:
        r.append(B.r_xnT[c])
    return r


def emit_setup(P, B):
    P.op("gpsimd", lambda e: e.reg_mov(B.RB_TOK, S - 1), writes=[])
    P.op("gpsimd", lambda e: e.reg_mov(B.RB_IDX, 8 * CAP - 1), writes=[])
    P.op("gpsimd", _memset(B.IDN[:], 0.0), writes=[B.r_idn])
    P.op("gpsimd", _asel(B.IDN[:], ALU.not_equal, 1.0, 0, [[-1, 128]], 1),
         reads=[B.r_idn], writes=[B.r_idn])
    P.op("gpsimd", _memset(B.ONES[:], 1.0), writes=[B.r_ones])
    P.op("gpsimd", _memset(B.EPSC[:], EPS), writes=[B.r_eps])
    P.op("gpsimd", _memset(B.INVD[:], 1.0 / D), writes=[B.r_eps])
    P.op("gpsimd", _memset(B.NEGH[:], -0.5), writes=[B.r_eps])
    P.op("gpsimd", _memset(B.MSK[:], 1.0), writes=[B.r_msk])
    P.op("gpsimd", _memset(B.SUB[:], 0.0), writes=[B.r_sub])
    P.op("gpsimd", _asel(B.SUB[:], ALU.is_ge, 1.0, 0, [[-1, 128]], 1),
         reads=[B.r_sub], writes=[B.r_sub])
    P.op("gpsimd", _memset(B.ONESF[:], 1.0), writes=[B.r_onesf])
    P.op("gpsimd", _memset(B.ONESR[:], 1.0), writes=[B.r_onesf])
    P.op("gpsimd", _iota(B.IOTA[:], [[128, 16]], 0, 1), writes=[B.r_iota])
    P.op("vector", _tcopy(B.IOTAF[:], B.IOTA[:]),
         reads=[B.r_iota], writes=[B.r_iota])
    for j in range(4):
        P.op("gpsimd", _memset(B.HB[:, j * 2048:(j + 1) * 2048], 0.0),
             writes=[B.r_hb[j]])
    for j in range(4):
        P.op("gpsimd", _asel(B.MSK[:, j * 512:(j + 1) * 512], ALU.is_ge, 0.0,
                             -j * 128, [[1, 512]], -1),
             reads=[B.r_msk], writes=[B.r_msk])


def emit_load_x(P, B, Xin, b):
    src = Xin[b].rearrange("(t p) d -> p t d", p=128)
    dst = B.XBUF[:].rearrange("p (t d) -> p t d", d=D)
    P.op("sync", _dma(dst, src), writes=B.r_x, dma=True, sem="spq")


def emit_store_x(P, B, Xout, b):
    dst = Xout[b].rearrange("(t p) d -> p t d", p=128)
    src = B.XBUF[:].rearrange("p (t d) -> p t d", d=D)
    P.op("sync", _dma(dst, src), reads=B.r_x, dma=True, sem="sst")


def emit_ln(P, B, XND=None):
    st = B.SMALL
    for g in range(2):
        for i in range(8):
            t = g * 8 + i
            xt = xtile(B, t)
            xtok = B.MB[:, i * 1024:(i + 1) * 1024]
            r_xtok = [B.r_mb[2 * i], B.r_mb[2 * i + 1]]
            par = t % 2
            r_st = B.r_stat[par]
            o = par * 8
            c_sum, c_ssq = st[:, o:o + 1], st[:, o + 1:o + 2]
            c_var = st[:, o + 3:o + 4]
            c_rstd, c_nmr = st[:, o + 4:o + 5], st[:, o + 5:o + 6]
            P.op("vector", _rsum(c_sum, xt), reads=[B.r_x[t], r_st],
                 writes=[r_st])
            P.op("scalar", _act(B.SCR[:], xt, AF.Square, accum=c_ssq),
                 reads=[B.r_x[t], r_st], writes=[B.r_scr, r_st])
            P.op("vector", _ts(c_var, c_sum, c_sum, None, ALU.mult),
                 reads=[r_st], writes=[r_st])
            P.op("vector", _ts(c_var, c_var, -1.0 / D, c_ssq, ALU.mult,
                               ALU.add), reads=[r_st], writes=[r_st])
            P.op("scalar", _act(c_rstd, c_var, AF.Ln, bias=B.EPSC[:],
                                scale=B.INVD[:]),
                 reads=[r_st, B.r_eps], writes=[r_st])
            P.op("scalar", _act(c_rstd, c_rstd, AF.Exp, scale=B.NEGH[:]),
                 reads=[r_st, B.r_eps], writes=[r_st])
            P.op("vector", _ts(c_nmr, c_sum, c_rstd, -1.0 / D, ALU.mult,
                               ALU.mult), reads=[r_st], writes=[r_st])
            P.op("vector", _ts(xtok, xt, c_rstd, c_nmr, ALU.mult, ALU.add),
                 reads=[B.r_x[t], r_st], writes=r_xtok)
        for c in range(8):
            bks = [B.r_bank[c % 2]]
            pb = bank(B, c % 2, 1, dt=BF16)
            grp = []
            for i in range(8):
                P.op("tensor",
                     _tr(pb[:, i * 128:(i + 1) * 128],
                         B.MB[:, i * 1024 + c * 128:i * 1024 + c * 128 + 128],
                         B.IDN[:]),
                     reads=[B.r_mb[2 * i], B.r_mb[2 * i + 1], B.r_idn],
                     writes=bks if i in (0, 7) else [],
                     group=grp, last=(i == 7))
            P.op("scalar", _cp(xnT(B, c)[:, g * 1024:(g + 1) * 1024], pb),
                 reads=bks,
                 writes=[B.r_xnTh[c] if g == 0 else B.r_xnT[c]])
        if XND is not None:
            P.op("sync", _dma(
                XND[g * 1024:(g + 1) * 1024].rearrange("(t p) d -> p t d",
                                                       p=128),
                B.MB[:].rearrange("p (t d) -> p t d", d=1024)),
                reads=[B.r_mb[j] for j in range(16)], writes=[B.r_xnd],
                dma=True, sem="spq")


def cast_load(P, B, dst_ap, src_ap, dst_res):
    P.op("gpsimd", _dma(dst_ap, src_ap), writes=[dst_res], dma=True,
         sem="gdq")


def emit_l0attn(P, B, W):
    wdq = W["L0_attn_W_dq"].rearrange("(c p) k -> c p k", p=128)
    wdkv = W["L0_attn_W_dkv"].rearrange("(c p) k -> c p k", p=128)
    wuq = W["L0_attn_W_uq"].rearrange("(c p) k -> c p k", p=128)
    wuk = W["L0_attn_W_uk"].rearrange("(c p) k -> c p k", p=128)
    wuv = W["L0_attn_W_uv"].rearrange("(c p) k -> c p k", p=128)
    wo = W["L0_attn_W_o"].rearrange("(c p) k -> c p k", p=128)
    for c in range(8):
        cast_load(P, B, B.WB1[:, c * 256:(c + 1) * 256], wdq[c], B.r_wb1)
        cast_load(P, B, B.WB1[:, 2048 + c * 256:2048 + (c + 1) * 256],
                  wdkv[c], B.r_wb1)
    for cc in range(2):
        cast_load(P, B, B.WB1[:, 4096 + cc * 1024:4096 + (cc + 1) * 1024],
                  wuq[cc], B.r_wb1)
        cast_load(P, B, B.WB1[:, 6144 + cc * 1024:6144 + (cc + 1) * 1024],
                  wuk[cc], B.r_wb1)
        cast_load(P, B, B.WB1[:, 8192 + cc * 1024:8192 + (cc + 1) * 1024],
                  wuv[cc], B.r_wb1)

    pp = 0
    for base_w, base_hb in ((0, 0), (2048, 4096)):
        for cc in range(2):
            for tc in range(4):
                bk = 4 + (pp % 2)
                pp += 1
                pb = bank(B, bk)
                grp = []
                for c in range(8):
                    wcol = base_w + c * 256 + cc * 128
                    P.op("tensor",
                         _mm(pb, B.WB1[:, wcol:wcol + 128],
                             xnT(B, c)[:, tc * 512:(tc + 1) * 512],
                             c == 0, c == 7),
                         reads=[B.r_wb1] + xnT_res(B, c, tc * 512,
                                                   (tc + 1) * 512),
                         writes=[B.r_bank[bk]] if c in (0, 7) else [],
                         group=grp, last=(c == 7))
                hcol = base_hb + cc * 2048 + tc * 512
                P.op("scalar", _cp(B.HB[:, hcol:hcol + 512], pb),
                     reads=[B.r_bank[bk]], writes=[B.r_hb[hcol // 2048]])

    for t in range(NT):
        P.op("gpsimd", _memset(
            B.WB2[:, t * 1040:(t + 1) * 1040].rearrange(
                "p (h f) -> p h f", f=65)[:, :, 64:65], 1.0),
            writes=[B.r_wb2])
    for t in range(NT):
        b0 = 4 + (t % 2) * 2
        pb = bank(B, b0, 2)
        bks = [B.r_bank[b0], B.r_bank[b0 + 1]]
        grp = []
        for cc in range(2):
            for dh in range(2):
                lcol = 4096 + cc * 2048 + t * 128
                rcol = 8192 + cc * 1024 + dh * 512
                P.op("tensor",
                     _mm(pb[:, dh * 512:(dh + 1) * 512],
                         B.HB[:, lcol:lcol + 128],
                         B.WB1[:, rcol:rcol + 512], cc == 0, cc == 1),
                     reads=[B.r_hb[2 + cc], B.r_wb1],
                     writes=bks if (cc, dh) in ((0, 0), (1, 1)) else [],
                     group=grp, last=(cc == 1 and dh == 1))
        P.op("vector", _tcopy(
            B.WB2[:, t * 1040:(t + 1) * 1040].rearrange(
                "p (h f) -> p h f", f=65)[:, :, 0:64],
            pb.rearrange("p (h f) -> p h f", f=64)),
            reads=bks, writes=[B.r_wb2])

    qt_slots = [B.r_mb[j] for j in range(4)]
    kt_slots = [B.r_mb[4 + j] for j in range(4)]
    pending_tail = None
    for p in range(8):
        for wbase, hb_base, mbase, slots, scale in (
                (4096, 0, 0, qt_slots, 0.125),
                (6144, 4096, 2048, kt_slots, None)):
            for tc in range(4):
                bk = 4 + (tc % 2)
                pb = bank(B, bk)
                grp = []
                for cc in range(2):
                    lcol = wbase + cc * 1024 + p * 128
                    rcol = hb_base + cc * 2048 + tc * 512
                    P.op("tensor",
                         _mm(pb, B.WB1[:, lcol:lcol + 128],
                             B.HB[:, rcol:rcol + 512], cc == 0, cc == 1),
                         reads=[B.r_wb1, B.r_hb[(hb_base + cc * 2048) // 2048]],
                         writes=[B.r_bank[bk]],
                         group=grp, last=(cc == 1))
                mcol = mbase + tc * 512
                if scale is None:
                    P.op("scalar", _cp(B.MB[:, mcol:mcol + 512], pb),
                         reads=[B.r_bank[bk]], writes=[slots[tc]])
                else:
                    P.op("scalar", _mulc(B.MB[:, mcol:mcol + 512], pb, scale),
                         reads=[B.r_bank[bk]], writes=[slots[tc]])
        if p == 7:
            for c in range(8):
                cast_load(P, B, B.WB1[:, c * 1024:(c + 1) * 1024], wo[c],
                          B.r_wb1)
        for hh in range(2):
            h = 2 * p + hh
            rs = hh * 64
            for qc in range(4):
                kn = 4 * (qc + 1)
                avb = 4 + ((h * 4 + qc) % 2)
                av = bank(B, avb, part=65)
                grp_av = []
                npr = kn // 2

                def emit_front(pr):
                    sbk = (pr % 2) * 2
                    spb = bank(B, sbk, 2)
                    pcol = (4096, 5120, 6656)[pr % 3]
                    pT2 = B.MB[:, pcol:pcol + 1024]
                    r_pT = [B.r_mb[pcol // 512], B.r_mb[pcol // 512 + 1]]
                    sgrp = []
                    for j in range(2):
                        kc = 2 * pr + j
                        P.op("tensor",
                             _mm(spb[:, j * 512:(j + 1) * 512],
                                 B.MB[rs:rs + 64,
                                      2048 + kc * 128:2048 + kc * 128 + 128],
                                 B.MB[rs:rs + 64, qc * 512:qc * 512 + 512],
                                 True, True),
                             reads=qt_slots + kt_slots,
                             writes=[B.r_bank[sbk], B.r_bank[sbk + 1]],
                             group=sgrp, last=(j == 1))
                    P.op("scalar", _act(pT2, spb, AF.Exp),
                         reads=[B.r_bank[sbk], B.r_bank[sbk + 1]] + r_pT,
                         writes=r_pT)
                    for j in range(2):
                        kc = 2 * pr + j
                        if kc >= qc * 4:
                            jj = kc - qc * 4
                            P.op("vector",
                                 _tt(pT2[:, j * 512:(j + 1) * 512],
                                     pT2[:, j * 512:(j + 1) * 512],
                                     B.MSK[:, jj * 512:jj * 512 + 512],
                                     ALU.mult),
                                 reads=r_pT + [B.r_msk], writes=r_pT)

                def emit_av(pr):
                    pcol = (4096, 5120, 6656)[pr % 3]
                    pT2 = B.MB[:, pcol:pcol + 1024]
                    r_pT = [B.r_mb[pcol // 512], B.r_mb[pcol // 512 + 1]]
                    for j in range(2):
                        kc = 2 * pr + j
                        vcol = kc * 1040 + h * 65
                        P.op("tensor",
                             _mm(av, B.WB2[:, vcol:vcol + 65],
                                 pT2[:, j * 512:(j + 1) * 512],
                                 kc == 0, kc == kn - 1),
                             reads=[B.r_wb2] + r_pT,
                             writes=[B.r_bank[avb]]
                             if kc in (0, kn - 1) else [],
                             group=grp_av, last=(j == 1))

                for pr in range(npr + 1):
                    if pr < npr:
                        emit_front(pr)
                    if pr >= 1:
                        emit_av(pr - 1)

                def tail(h=h, qc=qc, avb=avb, av=av, rs=rs):
                    P.op("vector", _rcpf(B.RCP[:], av[64:65, :]),
                         reads=[B.r_bank[avb], B.r_rcp], writes=[B.r_rcp])
                    bcb = 6 + (qc % 2)
                    bc = bank(B, bcb, part=64)
                    P.op("tensor", _mm(bc, B.ONES[:, 0:64], B.RCP[:], True,
                                       True),
                         reads=[B.r_ones, B.r_rcp], writes=[B.r_bank[bcb]])
                    P.op("scalar", _cp(B.SCR[0:64, 0:512], bc),
                         reads=[B.r_bank[bcb], B.r_scr], writes=[B.r_scr])
                    P.op("vector", _tt(
                        xnT(B, h // 2)[rs:rs + 64, qc * 512:qc * 512 + 512],
                        av[0:64, :], B.SCR[0:64, 0:512], ALU.mult),
                        reads=[B.r_bank[avb], B.r_scr],
                        writes=xnT_res(B, h // 2, qc * 512, qc * 512 + 512))

                tail()

    if pending_tail is not None:
        pending_tail()

    for t in range(NT):
        b0 = 4 + (t % 2) * 2
        pb = bank(B, b0, 2)
        bks = [B.r_bank[b0], B.r_bank[b0 + 1]]
        grp = []
        for c in range(8):
            for dh in range(2):
                rcol = c * 1024 + dh * 512
                P.op("tensor",
                     _mm(pb[:, dh * 512:(dh + 1) * 512],
                         xnT(B, c)[:, t * 128:(t + 1) * 128],
                         B.WB1[:, rcol:rcol + 512], c == 0, c == 7),
                     reads=xnT_res(B, c, t * 128, (t + 1) * 128) + [B.r_wb1],
                     writes=bks if (c, dh) in ((0, 0), (7, 1)) else [],
                     group=grp, last=(c == 7 and dh == 1))
        P.op("vector", _tt(xtile(B, t), xtile(B, t), pb, ALU.add),
             reads=bks + [B.r_x[t]], writes=[B.r_x[t]])


def emit_l0ffn(P, B, W):
    Wap = W["L0_ffn_weight"].rearrange("(c p) i -> c p i", p=128)
    for c in range(8):
        cast_load(P, B, B.HB[:, c * 1024:(c + 1) * 1024], Wap[c],
                  B.r_hb[c // 2])
    P.op("scalar", _act(B.HB[:], B.HB[:], AF.Sign),
         reads=B.r_hb, writes=B.r_hb)
    for c in range(8):
        bks = [B.r_bank[c % 2]]
        pb = bank(B, c % 2, 1, dt=BF16)
        grp = []
        for oc in range(8):
            scol = oc * 1024 + c * 128
            P.op("tensor",
                 _tr(pb[:, oc * 128:(oc + 1) * 128],
                     B.HB[:, scol:scol + 128], B.IDN[:]),
                 reads=[B.r_hb[oc // 2], B.r_idn],
                 writes=bks if oc in (0, 7) else [],
                 group=grp, last=(oc == 7))
        P.op("scalar", _cp(B.WB1[:, c * 1024:(c + 1) * 1024], pb),
             reads=bks, writes=[B.r_wb1])
    for t in range(NT):
        b0 = 4 + (t % 2) * 2
        pb = bank(B, b0, 2)
        bks = [B.r_bank[b0], B.r_bank[b0 + 1]]
        grp = []
        for c in range(8):
            for dh in range(2):
                rcol = c * 1024 + dh * 512
                P.op("tensor",
                     _mm(pb[:, dh * 512:(dh + 1) * 512],
                         xnT(B, c)[:, t * 128:(t + 1) * 128],
                         B.WB1[:, rcol:rcol + 512], c == 0, c == 7),
                     reads=xnT_res(B, c, t * 128, (t + 1) * 128) + [B.r_wb1],
                     writes=bks if (c, dh) in ((0, 0), (7, 1)) else [],
                     group=grp, last=(c == 7 and dh == 1))
        P.op("vector", _tt(xtile(B, t), xtile(B, t), pb, ALU.add),
             reads=bks + [B.r_x[t]], writes=[B.r_x[t]])


def emit_l1attn(P, B, W):
    wq = W["L1_attn_Wq"].rearrange("h (c p) k -> c p h k", p=128)
    wk = W["L1_attn_Wk"].rearrange("h (c p) k -> c p h k", p=128)
    wv = W["L1_attn_Wv"].rearrange("h (c p) k -> c p h k", p=128)

    def wb1_hk(col):
        return B.WB1[:, col:col + 1024].rearrange("p (h k) -> p h k", k=64)

    for c in range(8):
        cast_load(P, B, wb1_hk(c * 1024), wv[c], B.r_wb1)
    for t in range(NT):
        b0 = 4 + (t % 2) * 2
        pb = bank(B, b0, 2)
        bks = [B.r_bank[b0], B.r_bank[b0 + 1]]
        grp = []
        for c in range(8):
            for dh in range(2):
                rcol = c * 1024 + dh * 512
                P.op("tensor",
                     _mm(pb[:, dh * 512:(dh + 1) * 512],
                         xnT(B, c)[:, t * 128:(t + 1) * 128],
                         B.WB1[:, rcol:rcol + 512], c == 0, c == 7),
                     reads=xnT_res(B, c, t * 128, (t + 1) * 128) + [B.r_wb1],
                     writes=bks if (c, dh) in ((0, 0), (7, 1)) else [],
                     group=grp, last=(c == 7 and dh == 1))
        P.op("scalar", _mulc(B.WB2[:, t * 1024:(t + 1) * 1024], pb, 0.125),
             reads=bks, writes=[B.r_wb2])
    for c in range(8):
        cast_load(P, B, wb1_hk(c * 1024), wq[c], B.r_wb1)
        cast_load(P, B, wb1_hk(8192 + c * 1024), wk[c], B.r_wb1)

    p_bufs = [
        (B.MB[:, 0:2048], [B.r_mb[j] for j in range(4)]),
        (B.MB[:, 2048:4096], [B.r_mb[4 + j] for j in range(4)]),
        (B.HB[:, 4096:6144], [B.r_hb[2]]),
        (B.HB[:, 6144:8192], [B.r_hb[3]]),
    ]

    for p in range(8):
        for wbase, hbase, bki in ((0, 0, 0), (8192, 2048, 2)):
            for tc in range(4):
                bk = bki + (tc % 2)
                pb = bank(B, bk)
                grp = []
                for c in range(8):
                    lcol = wbase + c * 1024 + p * 128
                    P.op("tensor",
                         _mm(pb, B.WB1[:, lcol:lcol + 128],
                             xnT(B, c)[:, tc * 512:(tc + 1) * 512],
                             c == 0, c == 7),
                         reads=[B.r_wb1] + xnT_res(B, c, tc * 512,
                                                   (tc + 1) * 512),
                         writes=[B.r_bank[bk]] if c in (0, 7) else [],
                         group=grp, last=(c == 7))
                hcol = hbase + tc * 512
                P.op("scalar", _cp(B.HB[:, hcol:hcol + 512], pb),
                     reads=[B.r_bank[bk]], writes=[B.r_hb[hbase // 2048]])
        for hh in range(2):
            h = 2 * p + hh
            rs = hh * 64
            for qt in range(NT):
                par = qt % 2
                r_st = B.r_aux[par]
                o = par * 8
                stc = B.SMALL[:, 16 + o:16 + o + 8]
                nm = stc[:, 2:3]
                dS = stc[:, 5:6]
                sset = bank(B, par * 4, 4)
                sres = [B.r_bank[par * 4 + j] for j in range(4)]
                grp = []
                for kc in range(4):
                    P.op("tensor",
                         _mm(sset[:, kc * 512:(kc + 1) * 512],
                             B.HB[rs:rs + 64, qt * 128:qt * 128 + 128],
                             B.HB[rs:rs + 64,
                                  2048 + kc * 512:2048 + kc * 512 + 512],
                             True, True),
                         reads=[B.r_hb[0], B.r_hb[1]],
                         writes=sres,
                         group=grp, last=(kc == 3))
                P.op("vector", _rmaxn(nm, sset),
                     reads=sres + [r_st], writes=[r_st])
                pi = (h * NT + qt) % 4
                psb, r_p = p_bufs[pi]
                P.op("scalar", _act(psb, sset, AF.Exp, bias=nm, accum=dS),
                     reads=sres + [r_st] + r_p,
                     writes=r_p + [r_st])
                rc = B.SMALL[:, 32 + qt * 16 + h:32 + qt * 16 + h + 1]
                P.op("vector", _rcpf(rc, dS),
                     reads=[r_st, B.r_aux[2]], writes=[B.r_aux[2]])
                P.op("sync", _dma(B.PD[h][qt * 128:qt * 128 + 128, :], psb),
                     reads=r_p, writes=[B.r_pd[h]], dma=True, sem="spq")

    pt_bufs = [
        (B.MB[:, 4096:6144], [B.r_mb[8 + j] for j in range(4)]),
        (B.MB[:, 6144:8192], [B.r_mb[12 + j] for j in range(4)]),
    ]
    for h in range(H):
        avb = (h % 2) * 4
        av = B.PS[0:64, avb * 512:(avb + 4) * 512]
        avres = [B.r_bank[avb + j] for j in range(4)]
        grp_av = []
        for kc in range(NT):
            ptb, r_pt = pt_bufs[kc % 2]
            P.op("sync" if kc % 2 == 0 else "scalar",
                 _dma(ptb, B.PD[h][:, kc * 128:kc * 128 + 128],
                      transpose=True),
                 reads=[B.r_pd[h]] + r_pt, writes=r_pt, dma=True,
                 sem="spq" if kc % 2 == 0 else "scq")
            for qc in range(4):
                vcol = kc * 1024 + h * 64
                P.op("tensor",
                     _mm(av[:, qc * 512:(qc + 1) * 512],
                         B.WB2[:, vcol:vcol + 64],
                         ptb[:, qc * 512:(qc + 1) * 512],
                         kc == 0, kc == NT - 1),
                     reads=[B.r_wb2] + r_pt,
                     writes=avres
                     if (kc == 0 and qc == 0) or (kc == NT - 1 and qc == 3)
                     else [],
                     group=grp_av, last=(qc == 3))
        avst = B.SCR[0:64, :].bitcast(BF16)
        P.op("scalar", _cp(avst, av),
             reads=avres + [B.r_scr], writes=[B.r_scr])
        P.op("sync", _dma(B.AD[h * 64:h * 64 + 64, :], avst),
             reads=[B.r_scr], writes=[B.r_ad], dma=True, sem="spq")

    for t in range(NT):
        acol = (t % 2) * 1024
        at = B.MB[:, acol:acol + 1024]
        r_at = [B.r_mb[(t % 2) * 2], B.r_mb[(t % 2) * 2 + 1]]
        P.op("scalar", _dma(at, B.AD[:, t * 128:t * 128 + 128],
                            transpose=True),
             reads=[B.r_ad] + r_at, writes=r_at, dma=True, sem="scq")
        for h in range(H):
            rc = B.SMALL[:, 32 + t * 16 + h:32 + t * 16 + h + 1]
            P.op("vector", _ts(B.SCR[:, h * 64:h * 64 + 64],
                               at[:, h * 64:h * 64 + 64], rc, None, ALU.mult),
                 reads=r_at + [B.r_aux[2]], writes=[B.r_scr])
        P.op("vector", _tt(xtile(B, t), xtile(B, t), B.SCR[:], ALU.add),
             reads=[B.r_scr, B.r_x[t]], writes=[B.r_x[t]])


def emit_moe(P, B, W):
    rw = W["L1_ffn_router_w"].rearrange("(c p) e -> p c e", p=128)
    cast_load(P, B,
              B.MB[:, 6400:6464].rearrange("p (c e) -> p c e", e=E),
              rw, B.r_mb[12])
    P.op("gpsimd", _dma(B.RBSB[:], W["L1_ffn_router_b"].unsqueeze(0)),
         writes=[B.r_rb], dma=True, sem="gs2")
    P.op("gpsimd", _dma(B.B2SB[:], W["L1_ffn_b2"]),
         writes=[B.r_b2], dma=True, sem="gs1")

    st = B.SMALL
    for t in range(NT):
        lg = B.PS[:, 6 * 512:6 * 512 + 8]
        grp = []
        for c in range(8):
            P.op("tensor",
                 _mm(lg, xnT(B, c)[:, t * 128:(t + 1) * 128],
                     B.MB[:, 6400 + c * 8:6400 + c * 8 + 8], c == 0, False),
                 reads=xnT_res(B, c, t * 128, (t + 1) * 128) + [B.r_mb[12]],
                 writes=[B.r_bank[6]] if c == 0 else [],
                 group=grp, last=False)
        P.op("tensor", _mm(lg, B.ONES[:, 0:128], B.RBSB[:], False, True),
             reads=[B.r_ones, B.r_rb], writes=[B.r_bank[6]], group=grp,
             last=True)
        nm1, nm2 = st[:, 448:449], st[:, 449:450]
        den, rcp = st[:, 450:451], st[:, 451:452]
        m1 = B.SCR[:, 0:8]
        l2 = B.SCR[:, 8:16]
        e1 = B.SCR[:, 16:24]
        kp = B.SCR[:, 24:32]
        r_g = B.r_aux[5]
        P.op("vector", _rmaxn(nm1, lg),
             reads=[B.r_bank[6], r_g], writes=[r_g])
        P.op("vector", _ts(m1, lg, nm1, 0.0, ALU.add, ALU.is_ge),
             reads=[B.r_bank[6], r_g, B.r_scr], writes=[B.r_scr])
        P.op("vector", _ts(l2, m1, NEG, None, ALU.mult),
             reads=[B.r_scr], writes=[B.r_scr])
        P.op("vector", _tt(l2, l2, lg, ALU.add),
             reads=[B.r_scr, B.r_bank[6]], writes=[B.r_scr])
        P.op("vector", _rmaxn(nm2, l2),
             reads=[B.r_scr, r_g], writes=[r_g])
        P.op("scalar", _act(e1, lg, AF.Exp, bias=nm1),
             reads=[B.r_bank[6], r_g, B.r_scr], writes=[B.r_scr])
        P.op("vector", _ts(kp, lg, nm2, 0.0, ALU.add, ALU.is_ge),
             reads=[B.r_bank[6], r_g, B.r_scr], writes=[B.r_scr])
        P.op("vector", _tt(e1, e1, kp, ALU.mult),
             reads=[B.r_scr], writes=[B.r_scr])
        P.op("vector", _rsum(den, e1), reads=[B.r_scr, r_g], writes=[r_g])
        P.op("vector", _rcpf(rcp, den), reads=[r_g], writes=[r_g])
        P.op("vector", _ts(st[:, 288 + t * 8:288 + t * 8 + 8], e1, rcp, None,
                           ALU.mult),
             reads=[B.r_scr, r_g, B.r_aux[3]], writes=[B.r_aux[3]])
        P.op("vector", _tcopy(B.MB[:, 6144:6152],
                              st[:, 288 + t * 8:288 + t * 8 + 8]),
             reads=[B.r_aux[3], B.r_mb[12]], writes=[B.r_mb[12]])
        gt = bank(B, 7, dt=BF16)
        P.op("tensor", _tr(gt[0:8, 0:128], B.MB[:, 6144:6152], B.IDN[:]),
             reads=[B.r_mb[12], B.r_idn], writes=[B.r_bank[7]])
        P.op("scalar", _cp(B.MB[0:8, 6272:6400], gt[0:8, 0:128]),
             reads=[B.r_bank[7]], writes=[B.r_mb[12]])
        pb = bank(B, 2, 2)
        grpb = []
        for dh in range(2):
            P.op("tensor",
                 _mm(pb[:, dh * 512:(dh + 1) * 512], B.MB[0:8, 6272:6400],
                     B.B2SB[0:8, dh * 512:dh * 512 + 512], True, True),
                 reads=[B.r_mb[12], B.r_b2],
                 writes=[B.r_bank[2], B.r_bank[3]],
                 group=grpb, last=(dh == 1))
        P.op("vector", _tt(xtile(B, t), xtile(B, t), pb, ALU.add),
             reads=[B.r_bank[2], B.r_bank[3], B.r_x[t]], writes=[B.r_x[t]])

    w1 = W["L1_ffn_w1"]
    w2 = W["L1_ffn_w2"]
    b1 = W["L1_ffn_b1"]
    for ei in range(E):
        P.op("gpsimd", _dma(B.MB[0:32, 6464:6592],
                            b1[ei].rearrange("(c p) -> c p", p=128)),
             reads=[B.r_mb[12]], writes=[B.r_mb[12]], dma=True, sem="gs3")
        b1t = bank(B, 7, dt=BF16)
        P.op("tensor", _tr(b1t[0:128, 0:32], B.MB[0:32, 6464:6592],
                           B.IDN[0:32, 0:32]),
             reads=[B.r_mb[12], B.r_idn], writes=[B.r_bank[7]])
        P.op("scalar", _cp(st[:, 416:448], b1t[0:128, 0:32]),
             reads=[B.r_bank[7], B.r_aux[4]], writes=[B.r_aux[4]])
        for half in range(2):
            w1s = w1[ei][:, half * 2048:(half + 1) * 2048].rearrange(
                "(c p) h -> p c h", p=128)
            w2s = w2[ei][half * 2048:(half + 1) * 2048, :].rearrange(
                "(c p) d -> p c d", p=128)
            cast_load(P, B, B.WB1[:].rearrange("p (c h) -> p c h", c=8),
                      w1s, B.r_wb1)
            cast_load(P, B,
                      B.WB2[:, 0:16384].rearrange("p (c d) -> p c d", c=16),
                      w2s, B.r_wb2)
            for tc in range(4):
                for hc in range(16):
                    bk = hc % 2
                    pb = bank(B, bk)
                    grp = []
                    for c in range(8):
                        lcol = c * 2048 + hc * 128
                        P.op("tensor",
                             _mm(pb, B.WB1[:, lcol:lcol + 128],
                                 xnT(B, c)[:, tc * 512:(tc + 1) * 512],
                                 c == 0, c == 7),
                             reads=[B.r_wb1] + xnT_res(B, c, tc * 512,
                                                       (tc + 1) * 512),
                             writes=[B.r_bank[bk]] if c in (0, 7) else [],
                             group=grp, last=(c == 7))
                    bia = st[:, 416 + half * 16 + hc:417 + half * 16 + hc]
                    P.op("scalar", _act(B.HB[:, hc * 512:hc * 512 + 512], pb,
                                        AF.Relu, bias=bia),
                         reads=[B.r_bank[bk], B.r_aux[4]],
                         writes=[B.r_hb[hc // 4]])
                for tt in range(4):
                    t = tc * 4 + tt
                    b0 = 2 + (tt % 2) * 2
                    pb = bank(B, b0, 2)
                    bks = [B.r_bank[b0], B.r_bank[b0 + 1]]
                    grp = []
                    for hc in range(16):
                        for dh in range(2):
                            lcol = hc * 512 + tt * 128
                            rcol = hc * 1024 + dh * 512
                            P.op("tensor",
                                 _mm(pb[:, dh * 512:(dh + 1) * 512],
                                     B.HB[:, lcol:lcol + 128],
                                     B.WB2[:, rcol:rcol + 512],
                                     hc == 0, hc == 15),
                                 reads=[B.r_hb[hc // 4], B.r_wb2],
                                 writes=bks
                                 if (hc, dh) in ((0, 0), (15, 1)) else [],
                                 group=grp, last=(hc == 15 and dh == 1))
                    gcol = st[:, 288 + t * 8 + ei:288 + t * 8 + ei + 1]
                    P.op("vector", _ts(B.SCR[:], pb, gcol, None, ALU.mult),
                         reads=bks + [B.r_aux[3], B.r_scr],
                         writes=[B.r_scr])
                    P.op("vector", _tt(xtile(B, t), xtile(B, t), B.SCR[:],
                                       ALU.add),
                         reads=[B.r_scr, B.r_x[t]], writes=[B.r_x[t]])



def emit_moes(P, B, W, Xout, b):
    st = B.SMALL
    XOUTB = Xout[b]
    rw = W["L1_ffn_router_w"].rearrange("(c p) e -> p c e", p=128)
    cast_load(P, B,
              B.MB[:, 6400:6464].rearrange("p (c e) -> p c e", e=E),
              rw, B.r_mb[12])
    P.op("gpsimd", _dma(B.RBSB[:], W["L1_ffn_router_b"].unsqueeze(0)),
         writes=[B.r_rb], dma=True, sem="gs2")
    P.op("gpsimd", _dma(B.B2SB[:], W["L1_ffn_b2"]),
         writes=[B.r_b2], dma=True, sem="gs1")
    base = st[0:1, 452:460]
    cnt_s = st[0:1, 460:468]
    r_base = B.r_aux[6]
    P.op("gpsimd", _memset(base, 0.0), writes=[r_base])

    for t in range(NT):
        lg = B.PS[:, 6 * 512:6 * 512 + 8]
        grp = []
        for c in range(8):
            P.op("tensor",
                 _mm(lg, xnT(B, c)[:, t * 128:(t + 1) * 128],
                     B.MB[:, 6400 + c * 8:6400 + c * 8 + 8], c == 0, False),
                 reads=xnT_res(B, c, t * 128, (t + 1) * 128) + [B.r_mb[12]],
                 writes=[B.r_bank[6]] if c == 0 else [],
                 group=grp, last=False)
        P.op("tensor", _mm(lg, B.ONES[:, 0:128], B.RBSB[:], False, True),
             reads=[B.r_ones, B.r_rb], writes=[B.r_bank[6]], group=grp,
             last=True)
        nm1, nm2 = st[:, 448:449], st[:, 449:450]
        den, rcp = st[:, 450:451], st[:, 451:452]
        m1 = B.SCR[:, 0:8]
        l2 = B.SCR[:, 8:16]
        e1 = B.SCR[:, 16:24]
        kp = B.SCR[:, 24:32]
        nk = B.SCR[:, 32:40]
        posm = B.SCR[:, 40:48]
        r_g = B.r_aux[5]
        P.op("vector", _rmaxn(nm1, lg),
             reads=[B.r_bank[6], r_g], writes=[r_g])
        P.op("vector", _ts(m1, lg, nm1, 0.0, ALU.add, ALU.is_ge),
             reads=[B.r_bank[6], r_g, B.r_scr], writes=[B.r_scr])
        P.op("vector", _ts(l2, m1, NEG, None, ALU.mult),
             reads=[B.r_scr], writes=[B.r_scr])
        P.op("vector", _tt(l2, l2, lg, ALU.add),
             reads=[B.r_scr, B.r_bank[6]], writes=[B.r_scr])
        P.op("vector", _rmaxn(nm2, l2),
             reads=[B.r_scr, r_g], writes=[r_g])
        P.op("scalar", _act(e1, lg, AF.Exp, bias=nm1),
             reads=[B.r_bank[6], r_g, B.r_scr], writes=[B.r_scr])
        P.op("vector", _ts(kp, lg, nm2, 0.0, ALU.add, ALU.is_ge),
             reads=[B.r_bank[6], r_g, B.r_scr], writes=[B.r_scr])
        P.op("vector", _tt(e1, e1, kp, ALU.mult),
             reads=[B.r_scr], writes=[B.r_scr])
        P.op("vector", _rsum(den, e1), reads=[B.r_scr, r_g], writes=[r_g])
        P.op("vector", _rcpf(rcp, den), reads=[r_g], writes=[r_g])
        P.op("vector", _ts(st[:, 288 + t * 8:288 + t * 8 + 8], e1, rcp, None,
                           ALU.mult),
             reads=[B.r_scr, r_g, B.r_aux[3]], writes=[B.r_aux[3]])
        P.op("sync", _dma(B.GD[t * 128:(t + 1) * 128, :],
                          st[:, 288 + t * 8:288 + t * 8 + 8]),
             reads=[B.r_aux[3]], writes=[B.r_gd], dma=True, sem="spq")
        P.op("vector", _tcopy(B.MB[:, 6144:6152],
                              st[:, 288 + t * 8:288 + t * 8 + 8]),
             reads=[B.r_aux[3], B.r_mb[12]], writes=[B.r_mb[12]])
        gt = bank(B, 7, dt=BF16)
        P.op("tensor", _tr(gt[0:8, 0:128], B.MB[:, 6144:6152], B.IDN[:]),
             reads=[B.r_mb[12], B.r_idn], writes=[B.r_bank[7]])
        P.op("scalar", _cp(B.MB[0:8, 6272:6400], gt[0:8, 0:128]),
             reads=[B.r_bank[7]], writes=[B.r_mb[12]])
        pb = bank(B, 2, 2)
        grpb = []
        for dh in range(2):
            P.op("tensor",
                 _mm(pb[:, dh * 512:(dh + 1) * 512], B.MB[0:8, 6272:6400],
                     B.B2SB[0:8, dh * 512:dh * 512 + 512], True, True),
                 reads=[B.r_mb[12], B.r_b2],
                 writes=[B.r_bank[2], B.r_bank[3]],
                 group=grpb, last=(dh == 1))
        P.op("vector", _tt(xtile(B, t), xtile(B, t), pb, ALU.add),
             reads=[B.r_bank[2], B.r_bank[3], B.r_x[t]], writes=[B.r_x[t]])
        kpb = B.MB[:, 6152:6160]
        P.op("vector", _tcopy(kpb, kp),
             reads=[B.r_scr, B.r_mb[12]], writes=[B.r_mb[12]])
        pos = B.PS[:, 6 * 512 + 8:6 * 512 + 16]
        pgrp = []
        P.op("tensor", _mm(pos, B.SUB[:], kpb, True, False),
             reads=[B.r_sub, B.r_mb[12], B.r_bank[6]],
             writes=[B.r_bank[6]], group=pgrp, last=False)
        P.op("tensor", _mm(pos, B.ONESR[:], base, False, True),
             reads=[B.r_onesf, r_base], writes=[B.r_bank[6]], group=pgrp,
             last=True)
        cntp = B.PS[0:1, 7 * 512:7 * 512 + 8]
        P.op("tensor", _mm(cntp, B.ONESF[:], kp, True, True),
             reads=[B.r_onesf, B.r_scr, B.r_bank[7]], writes=[B.r_bank[7]])
        P.op("vector", _ts(nk, kp, -BIGIX, BIGIX, ALU.mult, ALU.add),
             reads=[B.r_scr], writes=[B.r_scr])
        P.op("vector", _tt(posm, pos, nk, ALU.add),
             reads=[B.r_bank[6], B.r_scr], writes=[B.r_scr])
        for e in range(E):
            P.op("vector", _ts(B.IDXW[:, e:e + 1], posm[:, e:e + 1],
                               float(e * CAP), None, ALU.add),
                 reads=[B.r_scr, B.r_idxw], writes=[B.r_idxw])
        for e in range(E):
            P.op("gpsimd", _iscatter(
                B.IDXD, B.IDXW[:, e:e + 1], B.IOTA[:, t:t + 1],
                B.RB_IDX),
                reads=[B.r_idxw, B.r_iota, B.r_idxd], writes=[B.r_idxd],
                dma=True, sem="gdq")
        P.op("vector", _tcopy(cnt_s, cntp),
             reads=[B.r_bank[7], r_base], writes=[r_base])
        P.op("vector", _tt(base, base, cnt_s, ALU.add),
             reads=[r_base], writes=[r_base])

    cbp = B.PS[:, 7 * 512 + 8:7 * 512 + 16]
    P.op("tensor", _mm(cbp, B.ONESR[:], base, True, True),
         reads=[B.r_onesf, r_base, B.r_bank[7]], writes=[B.r_bank[7]])
    P.op("vector", _tcopy(B.CNTB[:], cbp),
         reads=[B.r_bank[7], B.r_cntb], writes=[B.r_cntb])
    for g in range(6):
        P.op("vector", _ts(B.SMSK[:, g * 8:(g + 1) * 8], B.CNTB[:],
                           B.IOTAF[:, g:g + 1], 0.0, ALU.subtract, ALU.is_gt),
             reads=[B.r_cntb, B.r_iota, B.r_smsk], writes=[B.r_smsk])
    for e in range(E):
        for g in range(6):
            P.op("gpsimd", _dma(
                B.IDXS[:, e * 6 + g:e * 6 + g + 1],
                B.IDXD[e * CAP + g * 128:e * CAP + (g + 1) * 128, :]),
                reads=[B.r_idxd, B.r_idxs], writes=[B.r_idxs], dma=True,
                sem="gdq")
    for e in range(E):
        for g in range(6):
            P.op("gpsimd", _igather(
                B.GG[:, g * 8 + e:g * 8 + e + 1], B.GD,
                B.IDXS[:, e * 6 + g:e * 6 + g + 1], B.RB_TOK, eoff=e),
                reads=[B.r_idxs, B.r_gd, B.r_gg], writes=[B.r_gg],
                dma=True, sem="gdq")
    for g in range(6):
        P.op("vector", _tt(B.GG[:, g * 8:(g + 1) * 8],
                           B.GG[:, g * 8:(g + 1) * 8],
                           B.SMSK[:, g * 8:(g + 1) * 8], ALU.mult),
             reads=[B.r_smsk, B.r_gg], writes=[B.r_gg])

    P.op("sync", _dma(XOUTB.rearrange("(t p) d -> p t d", p=128),
                      B.XBUF[:].rearrange("p (t d) -> p t d", d=D)),
         reads=list(B.r_x) + [B.r_xout], writes=[B.r_xout], dma=True,
         sem="sst")

    w1 = W["L1_ffn_w1"]
    w2 = W["L1_ffn_w2"]
    b1 = W["L1_ffn_b1"]
    widths = ((0, 512), (512, 256))
    for ei in range(E):
        P.op("gpsimd", _dma(B.MB[0:32, 6464:6592],
                            b1[ei].rearrange("(c p) -> c p", p=128)),
             reads=[B.r_mb[12]], writes=[B.r_mb[12]], dma=True, sem="gs3")
        b1t = bank(B, 7, dt=BF16)
        P.op("tensor", _tr(b1t[0:128, 0:32], B.MB[0:32, 6464:6592],
                           B.IDN[0:32, 0:32]),
             reads=[B.r_mb[12], B.r_idn], writes=[B.r_bank[7]])
        P.op("scalar", _cp(st[:, 416:448], b1t[0:128, 0:32]),
             reads=[B.r_bank[7], B.r_aux[4]], writes=[B.r_aux[4]])
        for g in range(6):
            P.op("gpsimd", _igather(
                B.HB[:, g * 1024:(g + 1) * 1024], B.XND,
                B.IDXS[:, ei * 6 + g:ei * 6 + g + 1], B.RB_TOK),
                reads=[B.r_idxs, B.r_xnd, B.r_hb[g // 2]],
                writes=[B.r_hb[g // 2]], dma=True, sem="gdq")
        for c in range(8):
            bks = [B.r_bank[c % 2]]
            pb = bank(B, c % 2, 1, dt=BF16)
            grp = []
            for g in range(6):
                P.op("tensor",
                     _tr(pb[:, g * 128:(g + 1) * 128],
                         B.HB[:, g * 1024 + c * 128:g * 1024 + c * 128 + 128],
                         B.IDN[:]),
                     reads=[B.r_hb[g // 2], B.r_idn],
                     writes=bks if g in (0, 5) else [],
                     group=grp, last=(g == 5))
            P.op("scalar", _cp(B.ABUF[:, c * S:c * S + CAP], pb[:, 0:CAP]),
                 reads=bks, writes=[B.r_xnTh[c], B.r_xnT[c]])
        for half in range(2):
            w1s = w1[ei][:, half * 2048:(half + 1) * 2048].rearrange(
                "(c p) h -> p c h", p=128)
            w2s = w2[ei][half * 2048:(half + 1) * 2048, :].rearrange(
                "(c p) d -> p c d", p=128)
            cast_load(P, B, B.WB1[:].rearrange("p (c h) -> p c h", c=8),
                      w1s, B.r_wb1)
            cast_load(P, B,
                      B.WB2[:, 0:16384].rearrange("p (c d) -> p c d", c=16),
                      w2s, B.r_wb2)
            for off, wdt in widths:
                for hc in range(16):
                    bk = hc % 2
                    pb = bank(B, bk)[:, 0:wdt]
                    grp = []
                    for c in range(8):
                        lcol = c * 2048 + hc * 128
                        P.op("tensor",
                             _mm(pb, B.WB1[:, lcol:lcol + 128],
                                 B.ABUF[:, c * S + off:c * S + off + wdt],
                                 c == 0, c == 7),
                             reads=[B.r_wb1, B.r_xnTh[c], B.r_xnT[c]],
                             writes=[B.r_bank[bk]] if c in (0, 7) else [],
                             group=grp, last=(c == 7))
                    bia = st[:, 416 + half * 16 + hc:417 + half * 16 + hc]
                    P.op("scalar", _act(B.HB[:, hc * 512:hc * 512 + wdt], pb,
                                        AF.Relu, bias=bia),
                         reads=[B.r_bank[bk], B.r_aux[4]],
                         writes=[B.r_hb[hc // 4]])
                for tt in range(wdt // 128):
                    g = off // 128 + tt
                    b0 = 2 + (tt % 2) * 2
                    pb = bank(B, b0, 2)
                    bks = [B.r_bank[b0], B.r_bank[b0 + 1]]
                    grp = []
                    for hc in range(16):
                        for dh in range(2):
                            lcol = hc * 512 + tt * 128
                            rcol = hc * 1024 + dh * 512
                            P.op("tensor",
                                 _mm(pb[:, dh * 512:(dh + 1) * 512],
                                     B.HB[:, lcol:lcol + 128],
                                     B.WB2[:, rcol:rcol + 512],
                                     hc == 0, hc == 15),
                                 reads=[B.r_hb[hc // 4], B.r_wb2],
                                 writes=bks
                                 if (hc, dh) in ((0, 0), (15, 1)) else [],
                                 group=grp, last=(hc == 15 and dh == 1))
                    stg = (g % 2)
                    ystage = B.MB[:, stg * 2048:(stg + 1) * 2048].bitcast(F32)
                    r_y = [B.r_mb[stg * 4 + j] for j in range(4)]
                    gcol = B.GG[:, g * 8 + ei:g * 8 + ei + 1]
                    P.op("vector", _ts(ystage, pb, gcol, None, ALU.mult),
                         reads=bks + [B.r_gg] + r_y, writes=r_y)
                    P.op("gpsimd", _iscatter(
                        XOUTB, B.IDXS[:, ei * 6 + g:ei * 6 + g + 1],
                        ystage, B.RB_TOK, ALU.add),
                        reads=r_y + [B.r_idxs, B.r_xout], writes=[B.r_xout],
                        dma=True, sem="sct")

WEIGHT_SPECS = [
    ("L0_attn_W_dq", [D, DC]), ("L0_attn_W_uq", [DC, D]),
    ("L0_attn_W_dkv", [D, DC]), ("L0_attn_W_uk", [DC, D]),
    ("L0_attn_W_uv", [DC, D]), ("L0_attn_W_o", [D, D]),
    ("L0_ffn_weight", [D, D]),
    ("L1_attn_Wq", [H, D, DK]), ("L1_attn_Wk", [H, D, DK]),
    ("L1_attn_Wv", [H, D, DK]),
    ("L1_ffn_router_w", [D, E]), ("L1_ffn_router_b", [E]),
    ("L1_ffn_w1", [E, D, HID]), ("L1_ffn_b1", [E, HID]),
    ("L1_ffn_w2", [E, HID, D]), ("L1_ffn_b2", [E, D]),
]

PHASE_EMIT = {
    "l0attn": emit_l0attn,
    "l0ffn": emit_l0ffn,
    "l1attn": emit_l1attn,
    "moe": emit_moe,
}


def build_program(phases, n_batches=NB):
    import contextlib
    nc = bass.Bass("TRN2", target_bir_lowering=False)
    Xin = nc.dram_tensor("x", [n_batches, S, D], F32, kind="ExternalInput").ap()
    Xout = nc.dram_tensor("y", [n_batches, S, D], F32,
                          kind="ExternalOutput").ap()
    W = {}
    for name, shape in WEIGHT_SPECS:
        W[name] = nc.dram_tensor(name, shape, F32, kind="ExternalInput").ap()

    ctx = contextlib.ExitStack()
    with ctx:
        B = alloc_bufs(nc, ctx)
        B.RB_TOK = ctx.enter_context(nc.gpsimd.register("rb_tok"))
        B.RB_IDX = ctx.enter_context(nc.gpsimd.register("rb_idx"))
        B.PD = [nc.dram_tensor(f"PD{i}", [S, S], BF16).ap()
                for i in range(H)]
        B.XND = nc.dram_tensor("XND", [S, D], BF16).ap()
        B.GD = nc.dram_tensor("GD", [S, E], F32).ap()
        B.IDXD = nc.dram_tensor("IDXD", [E * CAP, 1], I32).ap()
        B.AD = nc.dram_tensor("AD", [D, S], BF16).ap()
        B.r_pd = [Res(f"pd{i}") for i in range(H)]
        B.r_ad = Res("ad")
        sems = {}
        for sname in ["pe", "ac", "dv", "po", "spq", "sst", "gdq", "scq",
                      "gs1", "gs2", "gs3", "sct"]:
            sems[sname] = ctx.enter_context(nc.semaphore(sname))
        P = Prog(nc)
        emit_setup(P, B)
        for b in range(n_batches):
            emit_load_x(P, B, Xin, b)
            for ph in phases:
                if ph == "moes":
                    emit_ln(P, B, XND=B.XND)
                    emit_moes(P, B, W, Xout, b)
                else:
                    emit_ln(P, B)
                    PHASE_EMIT[ph](P, B, W)
            if "moes" not in phases:
                emit_store_x(P, B, Xout, b)
        P.barrier_wait("sync", [P.now("sst"), P.now("sct"), P.now("gdq")])
        with nc.Block() as block:
            P.replay(block, sems)
    return nc


def _in_maps(inputs, n_cores=8):
    x = np.ascontiguousarray(inputs["L0_attn_x"], dtype=np.float32)
    maps = []
    for c in range(n_cores):
        m = {"x": x[c * NB:(c + 1) * NB]}
        for name, _ in WEIGHT_SPECS:
            m[name] = np.ascontiguousarray(inputs[name], dtype=np.float32)
        maps.append(m)
    return maps


def kernel(**inputs):
    nc = build_program(["l0attn", "l0ffn", "l1attn", "moe"])
    maps = _in_maps(inputs)
    res = run_bass_kernel_spmd(nc, maps, core_ids=list(range(8)))
    return np.concatenate([r["y"] for r in res.results], axis=0)


# version 09
# speedup vs baseline: 1.0644x; 1.0644x over previous
import numpy as np
import concourse.bass as bass
import concourse.mybir as mybir
from concourse.bass import IndirectOffsetOnAxis
from concourse.bass_utils import run_bass_kernel_spmd

F32 = mybir.dt.float32
BF16 = mybir.dt.bfloat16
AF = mybir.ActivationFunctionType
ALU = mybir.AluOpType
AX = mybir.AxisListType.X

D = 1024
S = 2048
NB = 4
H = 16
DK = 64
DC = 256
E = 8
HID = 4096
NT = S // 128
EPS = 1e-5
NEG = -1.0e30
CAP = 768
BIGIX = 1.0e8
I32 = mybir.dt.int32

ENGSEM = {"tensor": "pe", "scalar": "ac", "vector": "dv", "gpsimd": "po",
          "sync": "spq"}


def _mm(o, l, r, st, sp):
    def f(e):
        return e.matmul(o, l, r, start=st, stop=sp)
    return f


def _tr(o, s, idn):
    def f(e):
        return e.transpose(o, s, idn)
    return f


def _cp(o, s):
    def f(e):
        return e.copy(out=o, in_=s)
    return f


def _mulc(o, s, c):
    def f(e):
        return e.mul(out=o, in_=s, mul=c)
    return f


def _act(o, s, func, bias=None, accum=None, scale=None):
    def f(e):
        kw = {}
        if bias is not None:
            kw["bias"] = bias
        if accum is not None:
            kw["accum_out"] = accum
        if scale is not None:
            kw["scale"] = scale
        return e.activation(out=o, in_=s, func=func, **kw)
    return f


def _dma(o, s, transpose=False):
    def f(e):
        if transpose:
            return e.dma_start(out=o, in_=s, transpose=True)
        return e.dma_start(out=o, in_=s)
    return f


def _memset(ap, v):
    def f(e):
        return e.memset(ap, v)
    return f


def _asel(o, cmp, fill, base, pattern, cm):
    def f(e):
        return e.affine_select(out=o, in_=o, compare_op=cmp, fill=fill,
                               base=base, pattern=pattern,
                               channel_multiplier=cm)
    return f


def _rsum(o, s):
    def f(e):
        return e.reduce_sum(o, s, axis=AX)
    return f


def _rmaxn(o, s):
    def f(e):
        return e.reduce_max(o, s, axis=AX, negate=True)
    return f


def _ts(o, i0, s1, s2, op0, op1=None):
    def f(e):
        if op1 is None:
            return e.tensor_scalar(out=o, in0=i0, scalar1=s1, scalar2=None,
                                   op0=op0)
        return e.tensor_scalar(out=o, in0=i0, scalar1=s1, scalar2=s2,
                               op0=op0, op1=op1)
    return f


def _tt(o, i0, i1, op):
    def f(e):
        return e.tensor_tensor(out=o, in0=i0, in1=i1, op=op)
    return f


def _rcpf(o, s):
    def f(e):
        if o.dtype == BF16:
            with e.bass.allow_low_precision(
                    reason="attn denominator reciprocal cast to bf16 for the "
                           "broadcast matmul; ~0.4% scale noise, well inside "
                           "tolerance"):
                return e.reciprocal(o, s)
        return e.reciprocal(o, s)
    return f


def _tcopy(o, s):
    def f(e):
        return e.tensor_copy(out=o, in_=s)
    return f


def _igather(o, src, idx_ap, bound, eoff=0):
    def f(e):
        return e.indirect_dma_start(
            out=o, out_offset=None, in_=src,
            in_offset=IndirectOffsetOnAxis(idx_ap, 0),
            element_offset=eoff, bounds_check=bound, oob_is_err=False)
    return f


def _iscatter(o, idx_ap, src, bound, op=None):
    def f(e):
        return e.indirect_dma_start(
            out=o, out_offset=IndirectOffsetOnAxis(idx_ap, 0), in_=src,
            in_offset=None, bounds_check=bound, oob_is_err=False,
            **({"compute_op": op} if op is not None else {}))
    return f


def _iota(o, pattern, base, cm):
    def f(e):
        return e.iota(o, pattern, base=base, channel_multiplier=cm)
    return f


class Res:
    __slots__ = ("w", "r", "name")

    def __init__(self, name=""):
        self.w = None
        self.r = []
        self.name = name


class Evt:
    __slots__ = ("sem", "eager", "k", "needed", "final")

    def __init__(self, sem, eager, k):
        self.sem = sem
        self.eager = eager
        self.k = k
        self.needed = False
        self.final = None


class Prog:

    def __init__(self, nc):
        self.nc = nc
        self.ops = []
        self.C = {}

    def _bump(self, sem, k):
        self.C[sem] = self.C.get(sem, 0) + k
        return Evt(sem, self.C[sem], k)

    def op(self, engine, fn, reads=(), writes=(), sem=None, dma=False,
           group=None, last=True, extra=()):
        waits = {}

        def merge(evt):
            if evt is not None:
                s = evt.sem
                if s not in waits or waits[s].eager < evt.eager:
                    waits[s] = evt

        rr = list(reads)
        ww = list(writes)
        for r in rr:
            merge(r.w)
        for w in ww:
            merge(w.w)
            for ev in w.r:
                merge(ev)
        for ev in extra:
            merge(ev)

        semn = sem or ENGSEM[engine]
        k = 16 if dma else 1
        evt = None
        if last:
            evt = self._bump(semn, k)
        self.ops.append((engine, fn, waits, evt))

        if group is not None:
            group.extend(rr)
            if last:
                for r in group:
                    r.r.append(evt)
                for w in ww:
                    w.w = evt
                    w.r = []
                del group[:]
        else:
            assert last
            for r in rr:
                r.r.append(evt)
            for w in ww:
                w.w = evt
                w.r = []
        return evt

    def barrier_wait(self, engine, evts):
        waits = {}
        for evt in evts:
            s = evt.sem
            if s not in waits or waits[s].eager < evt.eager:
                waits[s] = evt
        self.ops.append((engine, None, waits, None))

    def now(self, sem):
        e = Evt(sem, self.C.get(sem, 0), 0)
        e.final = None
        for (_, _, _, evt) in reversed(self.ops):
            if evt is not None and evt.sem == sem:
                return evt
        return e

    def replay(self, block, sems):
        for (_, _, _, evt) in self.ops:
            if evt is not None and evt.k == 16:
                evt.needed = True
        for (_, _, waits, _) in self.ops:
            for evt in waits.values():
                evt.needed = True
        cnt = {}
        for (_, _, _, evt) in self.ops:
            if evt is not None and evt.needed:
                cnt[evt.sem] = cnt.get(evt.sem, 0) + evt.k
                evt.final = cnt[evt.sem]

        done = {}

        def run(eng_name):
            def body(e):
                for (en, fn, waits, evt) in self.ops:
                    if en != eng_name:
                        continue
                    for s in sorted(waits):
                        v = waits[s].final
                        if v is None:
                            continue
                        if done.get((eng_name, s), -1) >= v:
                            continue
                        e.wait_ge(sems[s], v)
                        done[(eng_name, s)] = v
                    if fn is None:
                        continue
                    inst = fn(e)
                    if evt is not None and evt.needed:
                        inst.then_inc(sems[evt.sem], evt.k)
            return body

        block.sync(run("sync"))
        block.scalar(run("scalar"))
        block.vector(run("vector"))
        block.tensor(run("tensor"))
        block.gpsimd(run("gpsimd"))


class Bufs:
    pass


def alloc_bufs(nc, ctx):
    B = Bufs()

    def sb(name, shape, dt):
        return ctx.enter_context(nc.sbuf_tensor(name, shape, dt))

    B.XBUF = sb("XBUF", [128, NT * D], F32)
    B.ABUF = sb("ABUF", [128, 8 * S], BF16)
    B.WB1 = sb("WB1", [128, 16384], BF16)
    B.WB2 = sb("WB2", [128, 16 * 1040], BF16)
    B.HB = sb("HB", [128, 8192], BF16)
    B.MB = sb("MB", [128, 8192], BF16)
    B.SMALL = sb("SMALL", [128, 480], F32)
    B.SCR = sb("SCR", [128, 1024], F32)
    B.IDN = sb("IDN", [128, 128], BF16)
    B.MSK = sb("MSK", [128, 2048], BF16)
    B.EPSC = sb("EPSC", [128, 1], F32)
    B.INVD = sb("INVD", [128, 1], F32)
    B.NEGH = sb("NEGH", [128, 1], F32)
    B.ONES = sb("ONES", [1, 128], BF16)
    B.SUB = sb("SUB", [128, 128], BF16)
    B.ONESF = sb("ONESF", [128, 1], F32)
    B.ONESR = sb("ONESR", [1, 128], F32)
    B.IOTA = sb("IOTA", [128, 16], I32)
    B.IOTAF = sb("IOTAF", [128, 16], F32)
    B.IDXW = sb("IDXW", [128, 8], I32)
    B.IDXS = sb("IDXS", [128, 48], I32)
    B.GG = sb("GG", [128, 48], F32)
    B.CNTB = sb("CNTB", [128, 8], F32)
    B.SMSK = sb("SMSK", [128, 48], F32)
    B.RCP = sb("RCP", [1, 512], BF16)
    B.B2SB = sb("B2SB", [8, 1024], BF16)
    B.RBSB = sb("RBSB", [1, 8], BF16)
    B.PS = ctx.enter_context(nc.psum_tensor("PS", [128, 4096], F32))

    B.r_x = [Res(f"x{t}") for t in range(NT)]
    B.r_xnT = [Res(f"xnT{c}") for c in range(8)]
    B.r_xnTh = [Res(f"xnTh{c}") for c in range(8)]
    B.r_wb1 = Res("wb1")
    B.r_wb2 = Res("wb2")
    B.r_hb = [Res(f"hb{j}") for j in range(4)]
    B.r_mb = [Res(f"mb{j}") for j in range(16)]
    B.r_msk = Res("msk")
    B.r_stat = [Res("stat0"), Res("stat1")]
    B.r_aux = [Res(f"aux{j}") for j in range(8)]
    B.r_scr = Res("scr")
    B.r_idn = Res("idn")
    B.r_eps = Res("eps")
    B.r_sub = Res("sub")
    B.r_onesf = Res("onesf")
    B.r_iota = Res("iota")
    B.r_idxw = Res("idxw")
    B.r_idxs = Res("idxs")
    B.r_gg = Res("gg")
    B.r_cntb = Res("cntb")
    B.r_smsk = Res("smsk")
    B.r_xout = Res("xout")
    B.r_xnd = Res("xnd")
    B.r_gd = Res("gd")
    B.r_idxd = Res("idxd")
    B.r_ones = Res("ones")
    B.r_rcp = Res("rcp")
    B.r_b2 = Res("b2")
    B.r_rb = Res("rb")
    B.r_bank = [Res(f"ps{b}") for b in range(8)]
    return B


def bank(B, i, n=1, part=128, dt=F32):
    ap = B.PS[0:part, i * 512:(i + n) * 512]
    return ap.bitcast(BF16) if dt is BF16 else ap


def xtile(B, t):
    return B.XBUF[:, t * D:(t + 1) * D]


def xnT(B, c):
    return B.ABUF[:, c * S:(c + 1) * S]


def xnT_res(B, c, lo, hi):
    r = []
    if lo < 1024:
        r.append(B.r_xnTh[c])
    if hi > 1024:
        r.append(B.r_xnT[c])
    return r


def emit_setup(P, B):
    P.op("gpsimd", lambda e: e.reg_mov(B.RB_TOK, S - 1), writes=[])
    P.op("gpsimd", lambda e: e.reg_mov(B.RB_IDX, 8 * CAP - 1), writes=[])
    P.op("gpsimd", _memset(B.IDN[:], 0.0), writes=[B.r_idn])
    P.op("gpsimd", _asel(B.IDN[:], ALU.not_equal, 1.0, 0, [[-1, 128]], 1),
         reads=[B.r_idn], writes=[B.r_idn])
    P.op("gpsimd", _memset(B.ONES[:], 1.0), writes=[B.r_ones])
    P.op("gpsimd", _memset(B.EPSC[:], EPS), writes=[B.r_eps])
    P.op("gpsimd", _memset(B.INVD[:], 1.0 / D), writes=[B.r_eps])
    P.op("gpsimd", _memset(B.NEGH[:], -0.5), writes=[B.r_eps])
    P.op("gpsimd", _memset(B.MSK[:], 1.0), writes=[B.r_msk])
    P.op("gpsimd", _memset(B.SUB[:], 0.0), writes=[B.r_sub])
    P.op("gpsimd", _asel(B.SUB[:], ALU.is_ge, 1.0, 0, [[-1, 128]], 1),
         reads=[B.r_sub], writes=[B.r_sub])
    P.op("gpsimd", _memset(B.ONESF[:], 1.0), writes=[B.r_onesf])
    P.op("gpsimd", _memset(B.ONESR[:], 1.0), writes=[B.r_onesf])
    P.op("gpsimd", _iota(B.IOTA[:], [[128, 16]], 0, 1), writes=[B.r_iota])
    P.op("vector", _tcopy(B.IOTAF[:], B.IOTA[:]),
         reads=[B.r_iota], writes=[B.r_iota])
    for j in range(4):
        P.op("gpsimd", _memset(B.HB[:, j * 2048:(j + 1) * 2048], 0.0),
             writes=[B.r_hb[j]])
    for j in range(4):
        P.op("gpsimd", _asel(B.MSK[:, j * 512:(j + 1) * 512], ALU.is_ge, 0.0,
                             -j * 128, [[1, 512]], -1),
             reads=[B.r_msk], writes=[B.r_msk])


def emit_load_x(P, B, Xin, b):
    src = Xin[b].rearrange("(t p) d -> p t d", p=128)
    dst = B.XBUF[:].rearrange("p (t d) -> p t d", d=D)
    P.op("sync", _dma(dst, src), writes=B.r_x, dma=True, sem="spq")


def emit_store_x(P, B, Xout, b):
    dst = Xout[b].rearrange("(t p) d -> p t d", p=128)
    src = B.XBUF[:].rearrange("p (t d) -> p t d", d=D)
    P.op("sync", _dma(dst, src), reads=B.r_x, dma=True, sem="sst")


def emit_ln(P, B, XND=None):
    st = B.SMALL
    for g in range(2):
        for i in range(8):
            t = g * 8 + i
            xt = xtile(B, t)
            xtok = B.MB[:, i * 1024:(i + 1) * 1024]
            r_xtok = [B.r_mb[2 * i], B.r_mb[2 * i + 1]]
            par = t % 2
            r_st = B.r_stat[par]
            o = par * 8
            c_sum, c_ssq = st[:, o:o + 1], st[:, o + 1:o + 2]
            c_var = st[:, o + 3:o + 4]
            c_rstd, c_nmr = st[:, o + 4:o + 5], st[:, o + 5:o + 6]
            P.op("vector", _rsum(c_sum, xt), reads=[B.r_x[t], r_st],
                 writes=[r_st])
            P.op("scalar", _act(B.SCR[:], xt, AF.Square, accum=c_ssq),
                 reads=[B.r_x[t], r_st], writes=[B.r_scr, r_st])
            P.op("vector", _ts(c_var, c_sum, c_sum, None, ALU.mult),
                 reads=[r_st], writes=[r_st])
            P.op("vector", _ts(c_var, c_var, -1.0 / D, c_ssq, ALU.mult,
                               ALU.add), reads=[r_st], writes=[r_st])
            P.op("scalar", _act(c_rstd, c_var, AF.Ln, bias=B.EPSC[:],
                                scale=B.INVD[:]),
                 reads=[r_st, B.r_eps], writes=[r_st])
            P.op("scalar", _act(c_rstd, c_rstd, AF.Exp, scale=B.NEGH[:]),
                 reads=[r_st, B.r_eps], writes=[r_st])
            P.op("vector", _ts(c_nmr, c_sum, c_rstd, -1.0 / D, ALU.mult,
                               ALU.mult), reads=[r_st], writes=[r_st])
            P.op("vector", _ts(xtok, xt, c_rstd, c_nmr, ALU.mult, ALU.add),
                 reads=[B.r_x[t], r_st], writes=r_xtok)
        for c in range(8):
            bks = [B.r_bank[c % 2]]
            pb = bank(B, c % 2, 1, dt=BF16)
            grp = []
            for i in range(8):
                P.op("tensor",
                     _tr(pb[:, i * 128:(i + 1) * 128],
                         B.MB[:, i * 1024 + c * 128:i * 1024 + c * 128 + 128],
                         B.IDN[:]),
                     reads=[B.r_mb[2 * i], B.r_mb[2 * i + 1], B.r_idn],
                     writes=bks if i in (0, 7) else [],
                     group=grp, last=(i == 7))
            P.op("scalar", _cp(xnT(B, c)[:, g * 1024:(g + 1) * 1024], pb),
                 reads=bks,
                 writes=[B.r_xnTh[c] if g == 0 else B.r_xnT[c]])
        if XND is not None:
            P.op("sync", _dma(
                XND[g * 1024:(g + 1) * 1024].rearrange("(t p) d -> p t d",
                                                       p=128),
                B.MB[:].rearrange("p (t d) -> p t d", d=1024)),
                reads=[B.r_mb[j] for j in range(16)], writes=[B.r_xnd],
                dma=True, sem="spq")


def cast_load(P, B, dst_ap, src_ap, dst_res):
    P.op("gpsimd", _dma(dst_ap, src_ap), writes=[dst_res], dma=True,
         sem="gdq")


def emit_l0attn(P, B, W):
    wdq = W["L0_attn_W_dq"].rearrange("(c p) k -> c p k", p=128)
    wdkv = W["L0_attn_W_dkv"].rearrange("(c p) k -> c p k", p=128)
    wuq = W["L0_attn_W_uq"].rearrange("(c p) k -> c p k", p=128)
    wuk = W["L0_attn_W_uk"].rearrange("(c p) k -> c p k", p=128)
    wuv = W["L0_attn_W_uv"].rearrange("(c p) k -> c p k", p=128)
    wo = W["L0_attn_W_o"].rearrange("(c p) k -> c p k", p=128)
    for c in range(8):
        cast_load(P, B, B.WB1[:, c * 256:(c + 1) * 256], wdq[c], B.r_wb1)
        cast_load(P, B, B.WB1[:, 2048 + c * 256:2048 + (c + 1) * 256],
                  wdkv[c], B.r_wb1)
    for cc in range(2):
        cast_load(P, B, B.WB1[:, 4096 + cc * 1024:4096 + (cc + 1) * 1024],
                  wuq[cc], B.r_wb1)
        cast_load(P, B, B.WB1[:, 6144 + cc * 1024:6144 + (cc + 1) * 1024],
                  wuk[cc], B.r_wb1)
        cast_load(P, B, B.WB1[:, 8192 + cc * 1024:8192 + (cc + 1) * 1024],
                  wuv[cc], B.r_wb1)

    pp = 0
    for base_w, base_hb in ((0, 0), (2048, 4096)):
        for cc in range(2):
            for tc in range(4):
                bk = 4 + (pp % 2)
                pp += 1
                pb = bank(B, bk)
                grp = []
                for c in range(8):
                    wcol = base_w + c * 256 + cc * 128
                    P.op("tensor",
                         _mm(pb, B.WB1[:, wcol:wcol + 128],
                             xnT(B, c)[:, tc * 512:(tc + 1) * 512],
                             c == 0, c == 7),
                         reads=[B.r_wb1] + xnT_res(B, c, tc * 512,
                                                   (tc + 1) * 512),
                         writes=[B.r_bank[bk]] if c in (0, 7) else [],
                         group=grp, last=(c == 7))
                hcol = base_hb + cc * 2048 + tc * 512
                P.op("scalar", _cp(B.HB[:, hcol:hcol + 512], pb),
                     reads=[B.r_bank[bk]], writes=[B.r_hb[hcol // 2048]])

    for t in range(NT):
        P.op("gpsimd", _memset(
            B.WB2[:, t * 1040:(t + 1) * 1040].rearrange(
                "p (h f) -> p h f", f=65)[:, :, 64:65], 1.0),
            writes=[B.r_wb2])
    for t in range(NT):
        b0 = 4 + (t % 2) * 2
        pb = bank(B, b0, 2)
        bks = [B.r_bank[b0], B.r_bank[b0 + 1]]
        grp = []
        for cc in range(2):
            for dh in range(2):
                lcol = 4096 + cc * 2048 + t * 128
                rcol = 8192 + cc * 1024 + dh * 512
                P.op("tensor",
                     _mm(pb[:, dh * 512:(dh + 1) * 512],
                         B.HB[:, lcol:lcol + 128],
                         B.WB1[:, rcol:rcol + 512], cc == 0, cc == 1),
                     reads=[B.r_hb[2 + cc], B.r_wb1],
                     writes=bks if (cc, dh) in ((0, 0), (1, 1)) else [],
                     group=grp, last=(cc == 1 and dh == 1))
        P.op("vector", _tcopy(
            B.WB2[:, t * 1040:(t + 1) * 1040].rearrange(
                "p (h f) -> p h f", f=65)[:, :, 0:64],
            pb.rearrange("p (h f) -> p h f", f=64)),
            reads=bks, writes=[B.r_wb2])

    qt_slots = [B.r_mb[j] for j in range(4)]
    kt_slots = [B.r_mb[4 + j] for j in range(4)]
    pending_tail = None
    for p in range(8):
        for wbase, hb_base, mbase, slots, scale in (
                (4096, 0, 0, qt_slots, 0.125),
                (6144, 4096, 2048, kt_slots, None)):
            for tc in range(4):
                bk = 4 + (tc % 2)
                pb = bank(B, bk)
                grp = []
                for cc in range(2):
                    lcol = wbase + cc * 1024 + p * 128
                    rcol = hb_base + cc * 2048 + tc * 512
                    P.op("tensor",
                         _mm(pb, B.WB1[:, lcol:lcol + 128],
                             B.HB[:, rcol:rcol + 512], cc == 0, cc == 1),
                         reads=[B.r_wb1, B.r_hb[(hb_base + cc * 2048) // 2048]],
                         writes=[B.r_bank[bk]],
                         group=grp, last=(cc == 1))
                mcol = mbase + tc * 512
                if scale is None:
                    P.op("scalar", _cp(B.MB[:, mcol:mcol + 512], pb),
                         reads=[B.r_bank[bk]], writes=[slots[tc]])
                else:
                    P.op("scalar", _mulc(B.MB[:, mcol:mcol + 512], pb, scale),
                         reads=[B.r_bank[bk]], writes=[slots[tc]])
        if p == 7:
            for c in range(8):
                cast_load(P, B, B.WB1[:, c * 1024:(c + 1) * 1024], wo[c],
                          B.r_wb1)
        for hh in range(2):
            h = 2 * p + hh
            rs = hh * 64
            for qc in range(4):
                kn = 4 * (qc + 1)
                avb = 4 + ((h * 4 + qc) % 2)
                av = bank(B, avb, part=65)
                grp_av = []
                npr = kn // 2

                def emit_front(pr):
                    sbk = (pr % 2) * 2
                    spb = bank(B, sbk, 2)
                    pcol = (4096, 5120, 6656)[pr % 3]
                    pT2 = B.MB[:, pcol:pcol + 1024]
                    r_pT = [B.r_mb[pcol // 512], B.r_mb[pcol // 512 + 1]]
                    sgrp = []
                    for j in range(2):
                        kc = 2 * pr + j
                        P.op("tensor",
                             _mm(spb[:, j * 512:(j + 1) * 512],
                                 B.MB[rs:rs + 64,
                                      2048 + kc * 128:2048 + kc * 128 + 128],
                                 B.MB[rs:rs + 64, qc * 512:qc * 512 + 512],
                                 True, True),
                             reads=qt_slots + kt_slots,
                             writes=[B.r_bank[sbk], B.r_bank[sbk + 1]],
                             group=sgrp, last=(j == 1))
                    P.op("scalar", _act(pT2, spb, AF.Exp),
                         reads=[B.r_bank[sbk], B.r_bank[sbk + 1]] + r_pT,
                         writes=r_pT)
                    for j in range(2):
                        kc = 2 * pr + j
                        if kc >= qc * 4:
                            jj = kc - qc * 4
                            P.op("vector",
                                 _tt(pT2[:, j * 512:(j + 1) * 512],
                                     pT2[:, j * 512:(j + 1) * 512],
                                     B.MSK[:, jj * 512:jj * 512 + 512],
                                     ALU.mult),
                                 reads=r_pT + [B.r_msk], writes=r_pT)

                def emit_av(pr):
                    pcol = (4096, 5120, 6656)[pr % 3]
                    pT2 = B.MB[:, pcol:pcol + 1024]
                    r_pT = [B.r_mb[pcol // 512], B.r_mb[pcol // 512 + 1]]
                    for j in range(2):
                        kc = 2 * pr + j
                        vcol = kc * 1040 + h * 65
                        P.op("tensor",
                             _mm(av, B.WB2[:, vcol:vcol + 65],
                                 pT2[:, j * 512:(j + 1) * 512],
                                 kc == 0, kc == kn - 1),
                             reads=[B.r_wb2] + r_pT,
                             writes=[B.r_bank[avb]]
                             if kc in (0, kn - 1) else [],
                             group=grp_av, last=(j == 1))

                for pr in range(npr + 1):
                    if pr < npr:
                        emit_front(pr)
                    if pr >= 1:
                        emit_av(pr - 1)

                def tail(h=h, qc=qc, avb=avb, av=av, rs=rs):
                    P.op("vector", _rcpf(B.RCP[:], av[64:65, :]),
                         reads=[B.r_bank[avb], B.r_rcp], writes=[B.r_rcp])
                    bcb = 6 + (qc % 2)
                    bc = bank(B, bcb, part=64)
                    P.op("tensor", _mm(bc, B.ONES[:, 0:64], B.RCP[:], True,
                                       True),
                         reads=[B.r_ones, B.r_rcp], writes=[B.r_bank[bcb]])
                    P.op("scalar", _cp(B.SCR[0:64, 0:512], bc),
                         reads=[B.r_bank[bcb], B.r_scr], writes=[B.r_scr])
                    P.op("vector", _tt(
                        xnT(B, h // 2)[rs:rs + 64, qc * 512:qc * 512 + 512],
                        av[0:64, :], B.SCR[0:64, 0:512], ALU.mult),
                        reads=[B.r_bank[avb], B.r_scr],
                        writes=xnT_res(B, h // 2, qc * 512, qc * 512 + 512))

                tail()

    if pending_tail is not None:
        pending_tail()

    for t in range(NT):
        b0 = 4 + (t % 2) * 2
        pb = bank(B, b0, 2)
        bks = [B.r_bank[b0], B.r_bank[b0 + 1]]
        grp = []
        for c in range(8):
            for dh in range(2):
                rcol = c * 1024 + dh * 512
                P.op("tensor",
                     _mm(pb[:, dh * 512:(dh + 1) * 512],
                         xnT(B, c)[:, t * 128:(t + 1) * 128],
                         B.WB1[:, rcol:rcol + 512], c == 0, c == 7),
                     reads=xnT_res(B, c, t * 128, (t + 1) * 128) + [B.r_wb1],
                     writes=bks if (c, dh) in ((0, 0), (7, 1)) else [],
                     group=grp, last=(c == 7 and dh == 1))
        P.op("vector", _tt(xtile(B, t), xtile(B, t), pb, ALU.add),
             reads=bks + [B.r_x[t]], writes=[B.r_x[t]])


def emit_l0ffn(P, B, W):
    Wap = W["L0_ffn_weight"].rearrange("(c p) i -> c p i", p=128)
    for c in range(8):
        cast_load(P, B, B.HB[:, c * 1024:(c + 1) * 1024], Wap[c],
                  B.r_hb[c // 2])
    P.op("scalar", _act(B.HB[:], B.HB[:], AF.Sign),
         reads=B.r_hb, writes=B.r_hb)
    for c in range(8):
        bks = [B.r_bank[c % 2]]
        pb = bank(B, c % 2, 1, dt=BF16)
        grp = []
        for oc in range(8):
            scol = oc * 1024 + c * 128
            P.op("tensor",
                 _tr(pb[:, oc * 128:(oc + 1) * 128],
                     B.HB[:, scol:scol + 128], B.IDN[:]),
                 reads=[B.r_hb[oc // 2], B.r_idn],
                 writes=bks if oc in (0, 7) else [],
                 group=grp, last=(oc == 7))
        P.op("scalar", _cp(B.WB1[:, c * 1024:(c + 1) * 1024], pb),
             reads=bks, writes=[B.r_wb1])
    for t in range(NT):
        b0 = 4 + (t % 2) * 2
        pb = bank(B, b0, 2)
        bks = [B.r_bank[b0], B.r_bank[b0 + 1]]
        grp = []
        for c in range(8):
            for dh in range(2):
                rcol = c * 1024 + dh * 512
                P.op("tensor",
                     _mm(pb[:, dh * 512:(dh + 1) * 512],
                         xnT(B, c)[:, t * 128:(t + 1) * 128],
                         B.WB1[:, rcol:rcol + 512], c == 0, c == 7),
                     reads=xnT_res(B, c, t * 128, (t + 1) * 128) + [B.r_wb1],
                     writes=bks if (c, dh) in ((0, 0), (7, 1)) else [],
                     group=grp, last=(c == 7 and dh == 1))
        P.op("vector", _tt(xtile(B, t), xtile(B, t), pb, ALU.add),
             reads=bks + [B.r_x[t]], writes=[B.r_x[t]])


def emit_l1attn(P, B, W):
    wq = W["L1_attn_Wq"].rearrange("h (c p) k -> c p h k", p=128)
    wk = W["L1_attn_Wk"].rearrange("h (c p) k -> c p h k", p=128)
    wv = W["L1_attn_Wv"].rearrange("h (c p) k -> c p h k", p=128)

    def wb1_hk(col):
        return B.WB1[:, col:col + 1024].rearrange("p (h k) -> p h k", k=64)

    for c in range(8):
        cast_load(P, B, wb1_hk(c * 1024), wv[c], B.r_wb1)
    for t in range(NT):
        b0 = 4 + (t % 2) * 2
        pb = bank(B, b0, 2)
        bks = [B.r_bank[b0], B.r_bank[b0 + 1]]
        grp = []
        for c in range(8):
            for dh in range(2):
                rcol = c * 1024 + dh * 512
                P.op("tensor",
                     _mm(pb[:, dh * 512:(dh + 1) * 512],
                         xnT(B, c)[:, t * 128:(t + 1) * 128],
                         B.WB1[:, rcol:rcol + 512], c == 0, c == 7),
                     reads=xnT_res(B, c, t * 128, (t + 1) * 128) + [B.r_wb1],
                     writes=bks if (c, dh) in ((0, 0), (7, 1)) else [],
                     group=grp, last=(c == 7 and dh == 1))
        P.op("scalar", _mulc(B.WB2[:, t * 1024:(t + 1) * 1024], pb, 0.125),
             reads=bks, writes=[B.r_wb2])
    for c in range(8):
        cast_load(P, B, wb1_hk(c * 1024), wq[c], B.r_wb1)
        cast_load(P, B, wb1_hk(8192 + c * 1024), wk[c], B.r_wb1)

    p_bufs = [
        (B.MB[:, 0:2048], [B.r_mb[j] for j in range(4)]),
        (B.MB[:, 2048:4096], [B.r_mb[4 + j] for j in range(4)]),
        (B.HB[:, 4096:6144], [B.r_hb[2]]),
        (B.HB[:, 6144:8192], [B.r_hb[3]]),
    ]

    for p in range(8):
        for wbase, hbase, bki in ((0, 0, 0), (8192, 2048, 2)):
            for tc in range(4):
                bk = bki + (tc % 2)
                pb = bank(B, bk)
                grp = []
                for c in range(8):
                    lcol = wbase + c * 1024 + p * 128
                    P.op("tensor",
                         _mm(pb, B.WB1[:, lcol:lcol + 128],
                             xnT(B, c)[:, tc * 512:(tc + 1) * 512],
                             c == 0, c == 7),
                         reads=[B.r_wb1] + xnT_res(B, c, tc * 512,
                                                   (tc + 1) * 512),
                         writes=[B.r_bank[bk]] if c in (0, 7) else [],
                         group=grp, last=(c == 7))
                hcol = hbase + tc * 512
                P.op("scalar", _cp(B.HB[:, hcol:hcol + 512], pb),
                     reads=[B.r_bank[bk]], writes=[B.r_hb[hbase // 2048]])
        for hh in range(2):
            h = 2 * p + hh
            rs = hh * 64
            for qt in range(NT):
                par = qt % 2
                r_st = B.r_aux[par]
                o = par * 8
                stc = B.SMALL[:, 16 + o:16 + o + 8]
                nm = stc[:, 2:3]
                dS = stc[:, 5:6]
                sset = bank(B, par * 4, 4)
                sres = [B.r_bank[par * 4 + j] for j in range(4)]
                grp = []
                for kc in range(4):
                    P.op("tensor",
                         _mm(sset[:, kc * 512:(kc + 1) * 512],
                             B.HB[rs:rs + 64, qt * 128:qt * 128 + 128],
                             B.HB[rs:rs + 64,
                                  2048 + kc * 512:2048 + kc * 512 + 512],
                             True, True),
                         reads=[B.r_hb[0], B.r_hb[1]],
                         writes=sres,
                         group=grp, last=(kc == 3))
                P.op("vector", _rmaxn(nm, sset),
                     reads=sres + [r_st], writes=[r_st])
                pi = (h * NT + qt) % 4
                psb, r_p = p_bufs[pi]
                P.op("scalar", _act(psb, sset, AF.Exp, bias=nm, accum=dS),
                     reads=sres + [r_st] + r_p,
                     writes=r_p + [r_st])
                rc = B.SMALL[:, 32 + qt * 16 + h:32 + qt * 16 + h + 1]
                P.op("vector", _rcpf(rc, dS),
                     reads=[r_st, B.r_aux[2]], writes=[B.r_aux[2]])
                P.op("sync", _dma(B.PD[h][qt * 128:qt * 128 + 128, :], psb),
                     reads=r_p, writes=[B.r_pd[h]], dma=True, sem="spq")

    pt_bufs = [
        (B.MB[:, 4096:6144], [B.r_mb[8 + j] for j in range(4)]),
        (B.MB[:, 6144:8192], [B.r_mb[12 + j] for j in range(4)]),
    ]
    for h in range(H):
        avb = (h % 2) * 4
        av = B.PS[0:64, avb * 512:(avb + 4) * 512]
        avres = [B.r_bank[avb + j] for j in range(4)]
        grp_av = []
        for kc in range(NT):
            ptb, r_pt = pt_bufs[kc % 2]
            P.op("sync" if kc % 2 == 0 else "scalar",
                 _dma(ptb, B.PD[h][:, kc * 128:kc * 128 + 128],
                      transpose=True),
                 reads=[B.r_pd[h]] + r_pt, writes=r_pt, dma=True,
                 sem="spq" if kc % 2 == 0 else "scq")
            for qc in range(4):
                vcol = kc * 1024 + h * 64
                P.op("tensor",
                     _mm(av[:, qc * 512:(qc + 1) * 512],
                         B.WB2[:, vcol:vcol + 64],
                         ptb[:, qc * 512:(qc + 1) * 512],
                         kc == 0, kc == NT - 1),
                     reads=[B.r_wb2] + r_pt,
                     writes=avres
                     if (kc == 0 and qc == 0) or (kc == NT - 1 and qc == 3)
                     else [],
                     group=grp_av, last=(qc == 3))
        avst = B.SCR[0:64, :].bitcast(BF16)
        P.op("scalar", _cp(avst, av),
             reads=avres + [B.r_scr], writes=[B.r_scr])
        P.op("sync", _dma(B.AD[h * 64:h * 64 + 64, :], avst),
             reads=[B.r_scr], writes=[B.r_ad], dma=True, sem="spq")

    for t in range(NT):
        acol = (t % 2) * 1024
        at = B.MB[:, acol:acol + 1024]
        r_at = [B.r_mb[(t % 2) * 2], B.r_mb[(t % 2) * 2 + 1]]
        P.op("scalar", _dma(at, B.AD[:, t * 128:t * 128 + 128],
                            transpose=True),
             reads=[B.r_ad] + r_at, writes=r_at, dma=True, sem="scq")
        for h in range(H):
            rc = B.SMALL[:, 32 + t * 16 + h:32 + t * 16 + h + 1]
            P.op("vector", _ts(B.SCR[:, h * 64:h * 64 + 64],
                               at[:, h * 64:h * 64 + 64], rc, None, ALU.mult),
                 reads=r_at + [B.r_aux[2]], writes=[B.r_scr])
        P.op("vector", _tt(xtile(B, t), xtile(B, t), B.SCR[:], ALU.add),
             reads=[B.r_scr, B.r_x[t]], writes=[B.r_x[t]])


def emit_moe(P, B, W):
    rw = W["L1_ffn_router_w"].rearrange("(c p) e -> p c e", p=128)
    cast_load(P, B,
              B.MB[:, 6400:6464].rearrange("p (c e) -> p c e", e=E),
              rw, B.r_mb[12])
    P.op("gpsimd", _dma(B.RBSB[:], W["L1_ffn_router_b"].unsqueeze(0)),
         writes=[B.r_rb], dma=True, sem="gs2")
    P.op("gpsimd", _dma(B.B2SB[:], W["L1_ffn_b2"]),
         writes=[B.r_b2], dma=True, sem="gs1")

    st = B.SMALL
    for t in range(NT):
        lg = B.PS[:, 6 * 512:6 * 512 + 8]
        grp = []
        for c in range(8):
            P.op("tensor",
                 _mm(lg, xnT(B, c)[:, t * 128:(t + 1) * 128],
                     B.MB[:, 6400 + c * 8:6400 + c * 8 + 8], c == 0, False),
                 reads=xnT_res(B, c, t * 128, (t + 1) * 128) + [B.r_mb[12]],
                 writes=[B.r_bank[6]] if c == 0 else [],
                 group=grp, last=False)
        P.op("tensor", _mm(lg, B.ONES[:, 0:128], B.RBSB[:], False, True),
             reads=[B.r_ones, B.r_rb], writes=[B.r_bank[6]], group=grp,
             last=True)
        nm1, nm2 = st[:, 448:449], st[:, 449:450]
        den, rcp = st[:, 450:451], st[:, 451:452]
        m1 = B.SCR[:, 0:8]
        l2 = B.SCR[:, 8:16]
        e1 = B.SCR[:, 16:24]
        kp = B.SCR[:, 24:32]
        r_g = B.r_aux[5]
        P.op("vector", _rmaxn(nm1, lg),
             reads=[B.r_bank[6], r_g], writes=[r_g])
        P.op("vector", _ts(m1, lg, nm1, 0.0, ALU.add, ALU.is_ge),
             reads=[B.r_bank[6], r_g, B.r_scr], writes=[B.r_scr])
        P.op("vector", _ts(l2, m1, NEG, None, ALU.mult),
             reads=[B.r_scr], writes=[B.r_scr])
        P.op("vector", _tt(l2, l2, lg, ALU.add),
             reads=[B.r_scr, B.r_bank[6]], writes=[B.r_scr])
        P.op("vector", _rmaxn(nm2, l2),
             reads=[B.r_scr, r_g], writes=[r_g])
        P.op("scalar", _act(e1, lg, AF.Exp, bias=nm1),
             reads=[B.r_bank[6], r_g, B.r_scr], writes=[B.r_scr])
        P.op("vector", _ts(kp, lg, nm2, 0.0, ALU.add, ALU.is_ge),
             reads=[B.r_bank[6], r_g, B.r_scr], writes=[B.r_scr])
        P.op("vector", _tt(e1, e1, kp, ALU.mult),
             reads=[B.r_scr], writes=[B.r_scr])
        P.op("vector", _rsum(den, e1), reads=[B.r_scr, r_g], writes=[r_g])
        P.op("vector", _rcpf(rcp, den), reads=[r_g], writes=[r_g])
        P.op("vector", _ts(st[:, 288 + t * 8:288 + t * 8 + 8], e1, rcp, None,
                           ALU.mult),
             reads=[B.r_scr, r_g, B.r_aux[3]], writes=[B.r_aux[3]])
        P.op("vector", _tcopy(B.MB[:, 6144:6152],
                              st[:, 288 + t * 8:288 + t * 8 + 8]),
             reads=[B.r_aux[3], B.r_mb[12]], writes=[B.r_mb[12]])
        gt = bank(B, 7, dt=BF16)
        P.op("tensor", _tr(gt[0:8, 0:128], B.MB[:, 6144:6152], B.IDN[:]),
             reads=[B.r_mb[12], B.r_idn], writes=[B.r_bank[7]])
        P.op("scalar", _cp(B.MB[0:8, 6272:6400], gt[0:8, 0:128]),
             reads=[B.r_bank[7]], writes=[B.r_mb[12]])
        pb = bank(B, 2, 2)
        grpb = []
        for dh in range(2):
            P.op("tensor",
                 _mm(pb[:, dh * 512:(dh + 1) * 512], B.MB[0:8, 6272:6400],
                     B.B2SB[0:8, dh * 512:dh * 512 + 512], True, True),
                 reads=[B.r_mb[12], B.r_b2],
                 writes=[B.r_bank[2], B.r_bank[3]],
                 group=grpb, last=(dh == 1))
        P.op("vector", _tt(xtile(B, t), xtile(B, t), pb, ALU.add),
             reads=[B.r_bank[2], B.r_bank[3], B.r_x[t]], writes=[B.r_x[t]])

    w1 = W["L1_ffn_w1"]
    w2 = W["L1_ffn_w2"]
    b1 = W["L1_ffn_b1"]
    for ei in range(E):
        P.op("gpsimd", _dma(B.MB[0:32, 6464:6592],
                            b1[ei].rearrange("(c p) -> c p", p=128)),
             reads=[B.r_mb[12]], writes=[B.r_mb[12]], dma=True, sem="gs3")
        b1t = bank(B, 7, dt=BF16)
        P.op("tensor", _tr(b1t[0:128, 0:32], B.MB[0:32, 6464:6592],
                           B.IDN[0:32, 0:32]),
             reads=[B.r_mb[12], B.r_idn], writes=[B.r_bank[7]])
        P.op("scalar", _cp(st[:, 416:448], b1t[0:128, 0:32]),
             reads=[B.r_bank[7], B.r_aux[4]], writes=[B.r_aux[4]])
        for half in range(2):
            w1s = w1[ei][:, half * 2048:(half + 1) * 2048].rearrange(
                "(c p) h -> p c h", p=128)
            w2s = w2[ei][half * 2048:(half + 1) * 2048, :].rearrange(
                "(c p) d -> p c d", p=128)
            cast_load(P, B, B.WB1[:].rearrange("p (c h) -> p c h", c=8),
                      w1s, B.r_wb1)
            cast_load(P, B,
                      B.WB2[:, 0:16384].rearrange("p (c d) -> p c d", c=16),
                      w2s, B.r_wb2)
            for tc in range(4):
                for hc in range(16):
                    bk = hc % 2
                    pb = bank(B, bk)
                    grp = []
                    for c in range(8):
                        lcol = c * 2048 + hc * 128
                        P.op("tensor",
                             _mm(pb, B.WB1[:, lcol:lcol + 128],
                                 xnT(B, c)[:, tc * 512:(tc + 1) * 512],
                                 c == 0, c == 7),
                             reads=[B.r_wb1] + xnT_res(B, c, tc * 512,
                                                       (tc + 1) * 512),
                             writes=[B.r_bank[bk]] if c in (0, 7) else [],
                             group=grp, last=(c == 7))
                    bia = st[:, 416 + half * 16 + hc:417 + half * 16 + hc]
                    P.op("scalar", _act(B.HB[:, hc * 512:hc * 512 + 512], pb,
                                        AF.Relu, bias=bia),
                         reads=[B.r_bank[bk], B.r_aux[4]],
                         writes=[B.r_hb[hc // 4]])
                for tt in range(4):
                    t = tc * 4 + tt
                    b0 = 2 + (tt % 2) * 2
                    pb = bank(B, b0, 2)
                    bks = [B.r_bank[b0], B.r_bank[b0 + 1]]
                    grp = []
                    for hc in range(16):
                        for dh in range(2):
                            lcol = hc * 512 + tt * 128
                            rcol = hc * 1024 + dh * 512
                            P.op("tensor",
                                 _mm(pb[:, dh * 512:(dh + 1) * 512],
                                     B.HB[:, lcol:lcol + 128],
                                     B.WB2[:, rcol:rcol + 512],
                                     hc == 0, hc == 15),
                                 reads=[B.r_hb[hc // 4], B.r_wb2],
                                 writes=bks
                                 if (hc, dh) in ((0, 0), (15, 1)) else [],
                                 group=grp, last=(hc == 15 and dh == 1))
                    gcol = st[:, 288 + t * 8 + ei:288 + t * 8 + ei + 1]
                    P.op("vector", _ts(B.SCR[:], pb, gcol, None, ALU.mult),
                         reads=bks + [B.r_aux[3], B.r_scr],
                         writes=[B.r_scr])
                    P.op("vector", _tt(xtile(B, t), xtile(B, t), B.SCR[:],
                                       ALU.add),
                         reads=[B.r_scr, B.r_x[t]], writes=[B.r_x[t]])



def emit_moes(P, B, W, Xout, b):
    st = B.SMALL
    XOUTB = Xout[b]
    rw = W["L1_ffn_router_w"].rearrange("(c p) e -> p c e", p=128)
    cast_load(P, B,
              B.MB[:, 6400:6464].rearrange("p (c e) -> p c e", e=E),
              rw, B.r_mb[12])
    P.op("gpsimd", _dma(B.RBSB[:], W["L1_ffn_router_b"].unsqueeze(0)),
         writes=[B.r_rb], dma=True, sem="gs2")
    P.op("gpsimd", _dma(B.B2SB[:], W["L1_ffn_b2"]),
         writes=[B.r_b2], dma=True, sem="gs1")
    base = st[0:1, 452:460]
    cnt_s = st[0:1, 460:468]
    r_base = B.r_aux[6]
    P.op("gpsimd", _memset(base, 0.0), writes=[r_base])

    for t in range(NT):
        lg = B.PS[:, 6 * 512:6 * 512 + 8]
        grp = []
        for c in range(8):
            P.op("tensor",
                 _mm(lg, xnT(B, c)[:, t * 128:(t + 1) * 128],
                     B.MB[:, 6400 + c * 8:6400 + c * 8 + 8], c == 0, False),
                 reads=xnT_res(B, c, t * 128, (t + 1) * 128) + [B.r_mb[12]],
                 writes=[B.r_bank[6]] if c == 0 else [],
                 group=grp, last=False)
        P.op("tensor", _mm(lg, B.ONES[:, 0:128], B.RBSB[:], False, True),
             reads=[B.r_ones, B.r_rb], writes=[B.r_bank[6]], group=grp,
             last=True)
        nm1, nm2 = st[:, 448:449], st[:, 449:450]
        den, rcp = st[:, 450:451], st[:, 451:452]
        m1 = B.SCR[:, 0:8]
        l2 = B.SCR[:, 8:16]
        e1 = B.SCR[:, 16:24]
        kp = B.SCR[:, 24:32]
        nk = B.SCR[:, 32:40]
        posm = B.SCR[:, 40:48]
        r_g = B.r_aux[5]
        P.op("vector", _rmaxn(nm1, lg),
             reads=[B.r_bank[6], r_g], writes=[r_g])
        P.op("vector", _ts(m1, lg, nm1, 0.0, ALU.add, ALU.is_ge),
             reads=[B.r_bank[6], r_g, B.r_scr], writes=[B.r_scr])
        P.op("vector", _ts(l2, m1, NEG, None, ALU.mult),
             reads=[B.r_scr], writes=[B.r_scr])
        P.op("vector", _tt(l2, l2, lg, ALU.add),
             reads=[B.r_scr, B.r_bank[6]], writes=[B.r_scr])
        P.op("vector", _rmaxn(nm2, l2),
             reads=[B.r_scr, r_g], writes=[r_g])
        P.op("scalar", _act(e1, lg, AF.Exp, bias=nm1),
             reads=[B.r_bank[6], r_g, B.r_scr], writes=[B.r_scr])
        P.op("vector", _ts(kp, lg, nm2, 0.0, ALU.add, ALU.is_ge),
             reads=[B.r_bank[6], r_g, B.r_scr], writes=[B.r_scr])
        P.op("vector", _tt(e1, e1, kp, ALU.mult),
             reads=[B.r_scr], writes=[B.r_scr])
        P.op("vector", _rsum(den, e1), reads=[B.r_scr, r_g], writes=[r_g])
        P.op("vector", _rcpf(rcp, den), reads=[r_g], writes=[r_g])
        P.op("vector", _ts(st[:, 288 + t * 8:288 + t * 8 + 8], e1, rcp, None,
                           ALU.mult),
             reads=[B.r_scr, r_g, B.r_aux[3]], writes=[B.r_aux[3]])
        P.op("sync", _dma(B.GD[t * 128:(t + 1) * 128, :],
                          st[:, 288 + t * 8:288 + t * 8 + 8]),
             reads=[B.r_aux[3]], writes=[B.r_gd], dma=True, sem="spq")
        P.op("vector", _tcopy(B.MB[:, 6144:6152],
                              st[:, 288 + t * 8:288 + t * 8 + 8]),
             reads=[B.r_aux[3], B.r_mb[12]], writes=[B.r_mb[12]])
        gt = bank(B, 7, dt=BF16)
        P.op("tensor", _tr(gt[0:8, 0:128], B.MB[:, 6144:6152], B.IDN[:]),
             reads=[B.r_mb[12], B.r_idn], writes=[B.r_bank[7]])
        P.op("scalar", _cp(B.MB[0:8, 6272:6400], gt[0:8, 0:128]),
             reads=[B.r_bank[7]], writes=[B.r_mb[12]])
        pb = bank(B, 2, 2)
        grpb = []
        for dh in range(2):
            P.op("tensor",
                 _mm(pb[:, dh * 512:(dh + 1) * 512], B.MB[0:8, 6272:6400],
                     B.B2SB[0:8, dh * 512:dh * 512 + 512], True, True),
                 reads=[B.r_mb[12], B.r_b2],
                 writes=[B.r_bank[2], B.r_bank[3]],
                 group=grpb, last=(dh == 1))
        P.op("vector", _tt(xtile(B, t), xtile(B, t), pb, ALU.add),
             reads=[B.r_bank[2], B.r_bank[3], B.r_x[t]], writes=[B.r_x[t]])
        kpb = B.MB[:, 6152:6160]
        P.op("vector", _tcopy(kpb, kp),
             reads=[B.r_scr, B.r_mb[12]], writes=[B.r_mb[12]])
        pos = B.PS[:, 6 * 512 + 8:6 * 512 + 16]
        pgrp = []
        P.op("tensor", _mm(pos, B.SUB[:], kpb, True, False),
             reads=[B.r_sub, B.r_mb[12], B.r_bank[6]],
             writes=[B.r_bank[6]], group=pgrp, last=False)
        P.op("tensor", _mm(pos, B.ONESR[:], base, False, True),
             reads=[B.r_onesf, r_base], writes=[B.r_bank[6]], group=pgrp,
             last=True)
        cntp = B.PS[0:1, 7 * 512:7 * 512 + 8]
        P.op("tensor", _mm(cntp, B.ONESF[:], kp, True, True),
             reads=[B.r_onesf, B.r_scr, B.r_bank[7]], writes=[B.r_bank[7]])
        P.op("vector", _ts(nk, kp, -BIGIX, BIGIX, ALU.mult, ALU.add),
             reads=[B.r_scr], writes=[B.r_scr])
        P.op("vector", _tt(posm, pos, nk, ALU.add),
             reads=[B.r_bank[6], B.r_scr], writes=[B.r_scr])
        for e in range(E):
            P.op("vector", _ts(B.IDXW[:, e:e + 1], posm[:, e:e + 1],
                               float(e * CAP), None, ALU.add),
                 reads=[B.r_scr, B.r_idxw], writes=[B.r_idxw])
        for e in range(E):
            P.op("gpsimd", _iscatter(
                B.IDXD, B.IDXW[:, e:e + 1], B.IOTA[:, t:t + 1],
                B.RB_IDX),
                reads=[B.r_idxw, B.r_iota, B.r_idxd], writes=[B.r_idxd],
                dma=True, sem="gdq")
        P.op("vector", _tcopy(cnt_s, cntp),
             reads=[B.r_bank[7], r_base], writes=[r_base])
        P.op("vector", _tt(base, base, cnt_s, ALU.add),
             reads=[r_base], writes=[r_base])

    cbp = B.PS[:, 7 * 512 + 8:7 * 512 + 16]
    P.op("tensor", _mm(cbp, B.ONESR[:], base, True, True),
         reads=[B.r_onesf, r_base, B.r_bank[7]], writes=[B.r_bank[7]])
    P.op("vector", _tcopy(B.CNTB[:], cbp),
         reads=[B.r_bank[7], B.r_cntb], writes=[B.r_cntb])
    for g in range(6):
        P.op("vector", _ts(B.SMSK[:, g * 8:(g + 1) * 8], B.CNTB[:],
                           B.IOTAF[:, g:g + 1], 0.0, ALU.subtract, ALU.is_gt),
             reads=[B.r_cntb, B.r_iota, B.r_smsk], writes=[B.r_smsk])
    for e in range(E):
        for g in range(6):
            P.op("gpsimd", _dma(
                B.IDXS[:, e * 6 + g:e * 6 + g + 1],
                B.IDXD[e * CAP + g * 128:e * CAP + (g + 1) * 128, :]),
                reads=[B.r_idxd, B.r_idxs], writes=[B.r_idxs], dma=True,
                sem="gdq")
    for e in range(E):
        for g in range(6):
            P.op("gpsimd", _igather(
                B.GG[:, g * 8 + e:g * 8 + e + 1], B.GD,
                B.IDXS[:, e * 6 + g:e * 6 + g + 1], B.RB_TOK, eoff=e),
                reads=[B.r_idxs, B.r_gd, B.r_gg], writes=[B.r_gg],
                dma=True, sem="gdq")
    for g in range(6):
        P.op("vector", _tt(B.GG[:, g * 8:(g + 1) * 8],
                           B.GG[:, g * 8:(g + 1) * 8],
                           B.SMSK[:, g * 8:(g + 1) * 8], ALU.mult),
             reads=[B.r_smsk, B.r_gg], writes=[B.r_gg])

    P.op("sync", _dma(XOUTB.rearrange("(t p) d -> p t d", p=128),
                      B.XBUF[:].rearrange("p (t d) -> p t d", d=D)),
         reads=list(B.r_x) + [B.r_xout], writes=[B.r_xout], dma=True,
         sem="sst")

    w1 = W["L1_ffn_w1"]
    w2 = W["L1_ffn_w2"]
    b1 = W["L1_ffn_b1"]
    widths = ((0, 512), (512, 256))
    for ei in range(E):
        P.op("gpsimd", _dma(B.MB[0:32, 6464:6592],
                            b1[ei].rearrange("(c p) -> c p", p=128)),
             reads=[B.r_mb[12]], writes=[B.r_mb[12]], dma=True, sem="gs3")
        b1t = bank(B, 7, dt=BF16)
        P.op("tensor", _tr(b1t[0:128, 0:32], B.MB[0:32, 6464:6592],
                           B.IDN[0:32, 0:32]),
             reads=[B.r_mb[12], B.r_idn], writes=[B.r_bank[7]])
        P.op("scalar", _cp(st[:, 416:448], b1t[0:128, 0:32]),
             reads=[B.r_bank[7], B.r_aux[4]], writes=[B.r_aux[4]])
        for g in range(6):
            P.op("gpsimd", _igather(
                B.HB[:, g * 1024:(g + 1) * 1024], B.XND,
                B.IDXS[:, ei * 6 + g:ei * 6 + g + 1], B.RB_TOK),
                reads=[B.r_idxs, B.r_xnd, B.r_hb[g // 2]],
                writes=[B.r_hb[g // 2]], dma=True, sem="gdq")
        for c in range(8):
            bks = [B.r_bank[c % 2]]
            pb = bank(B, c % 2, 1, dt=BF16)
            grp = []
            for g in range(6):
                P.op("tensor",
                     _tr(pb[:, g * 128:(g + 1) * 128],
                         B.HB[:, g * 1024 + c * 128:g * 1024 + c * 128 + 128],
                         B.IDN[:]),
                     reads=[B.r_hb[g // 2], B.r_idn],
                     writes=bks if g in (0, 5) else [],
                     group=grp, last=(g == 5))
            P.op("scalar", _cp(B.ABUF[:, c * S:c * S + CAP], pb[:, 0:CAP]),
                 reads=bks, writes=[B.r_xnTh[c], B.r_xnT[c]])
        for half in range(2):
            w1s = w1[ei][:, half * 2048:(half + 1) * 2048].rearrange(
                "(c p) h -> p c h", p=128)
            w2s = w2[ei][half * 2048:(half + 1) * 2048, :].rearrange(
                "(c p) d -> p c d", p=128)
            cast_load(P, B, B.WB1[:].rearrange("p (c h) -> p c h", c=8),
                      w1s, B.r_wb1)
            cast_load(P, B,
                      B.WB2[:, 0:16384].rearrange("p (c d) -> p c d", c=16),
                      w2s, B.r_wb2)
            for off, wdt in widths:
                for hc in range(16):
                    bk = hc % 2
                    pb = bank(B, bk)[:, 0:wdt]
                    grp = []
                    for c in range(8):
                        lcol = c * 2048 + hc * 128
                        P.op("tensor",
                             _mm(pb, B.WB1[:, lcol:lcol + 128],
                                 B.ABUF[:, c * S + off:c * S + off + wdt],
                                 c == 0, c == 7),
                             reads=[B.r_wb1, B.r_xnTh[c], B.r_xnT[c]],
                             writes=[B.r_bank[bk]] if c in (0, 7) else [],
                             group=grp, last=(c == 7))
                    bia = st[:, 416 + half * 16 + hc:417 + half * 16 + hc]
                    P.op("scalar", _act(B.HB[:, hc * 512:hc * 512 + wdt], pb,
                                        AF.Relu, bias=bia),
                         reads=[B.r_bank[bk], B.r_aux[4]],
                         writes=[B.r_hb[hc // 4]])
                for tt in range(wdt // 128):
                    g = off // 128 + tt
                    b0 = 2 + (tt % 2) * 2
                    pb = bank(B, b0, 2)
                    bks = [B.r_bank[b0], B.r_bank[b0 + 1]]
                    grp = []
                    for hc in range(16):
                        for dh in range(2):
                            lcol = hc * 512 + tt * 128
                            rcol = hc * 1024 + dh * 512
                            P.op("tensor",
                                 _mm(pb[:, dh * 512:(dh + 1) * 512],
                                     B.HB[:, lcol:lcol + 128],
                                     B.WB2[:, rcol:rcol + 512],
                                     hc == 0, hc == 15),
                                 reads=[B.r_hb[hc // 4], B.r_wb2],
                                 writes=bks
                                 if (hc, dh) in ((0, 0), (15, 1)) else [],
                                 group=grp, last=(hc == 15 and dh == 1))
                    stg = (g % 2)
                    ystage = B.MB[:, stg * 2048:(stg + 1) * 2048].bitcast(F32)
                    r_y = [B.r_mb[stg * 4 + j] for j in range(4)]
                    gcol = B.GG[:, g * 8 + ei:g * 8 + ei + 1]
                    P.op("vector", _ts(ystage, pb, gcol, None, ALU.mult),
                         reads=bks + [B.r_gg] + r_y, writes=r_y)
                    P.op("gpsimd", _iscatter(
                        XOUTB, B.IDXS[:, ei * 6 + g:ei * 6 + g + 1],
                        ystage, B.RB_TOK, ALU.add),
                        reads=r_y + [B.r_idxs, B.r_xout], writes=[B.r_xout],
                        dma=True, sem="sct")

WEIGHT_SPECS = [
    ("L0_attn_W_dq", [D, DC]), ("L0_attn_W_uq", [DC, D]),
    ("L0_attn_W_dkv", [D, DC]), ("L0_attn_W_uk", [DC, D]),
    ("L0_attn_W_uv", [DC, D]), ("L0_attn_W_o", [D, D]),
    ("L0_ffn_weight", [D, D]),
    ("L1_attn_Wq", [H, D, DK]), ("L1_attn_Wk", [H, D, DK]),
    ("L1_attn_Wv", [H, D, DK]),
    ("L1_ffn_router_w", [D, E]), ("L1_ffn_router_b", [E]),
    ("L1_ffn_w1", [E, D, HID]), ("L1_ffn_b1", [E, HID]),
    ("L1_ffn_w2", [E, HID, D]), ("L1_ffn_b2", [E, D]),
]

PHASE_EMIT = {
    "l0attn": emit_l0attn,
    "l0ffn": emit_l0ffn,
    "l1attn": emit_l1attn,
    "moe": emit_moe,
}


def build_program(phases, n_batches=NB):
    import contextlib
    nc = bass.Bass("TRN2", target_bir_lowering=False)
    Xin = nc.dram_tensor("x", [n_batches, S, D], F32, kind="ExternalInput").ap()
    Xout = nc.dram_tensor("y", [n_batches, S, D], F32,
                          kind="ExternalOutput").ap()
    W = {}
    for name, shape in WEIGHT_SPECS:
        W[name] = nc.dram_tensor(name, shape, F32, kind="ExternalInput").ap()

    ctx = contextlib.ExitStack()
    with ctx:
        B = alloc_bufs(nc, ctx)
        B.RB_TOK = ctx.enter_context(nc.gpsimd.register("rb_tok"))
        B.RB_IDX = ctx.enter_context(nc.gpsimd.register("rb_idx"))
        B.PD = [nc.dram_tensor(f"PD{i}", [S, S], BF16).ap()
                for i in range(H)]
        B.XND = nc.dram_tensor("XND", [S, D], BF16).ap()
        B.GD = nc.dram_tensor("GD", [S, E], F32).ap()
        B.IDXD = nc.dram_tensor("IDXD", [E * CAP, 1], I32).ap()
        B.AD = nc.dram_tensor("AD", [D, S], BF16).ap()
        B.r_pd = [Res(f"pd{i}") for i in range(H)]
        B.r_ad = Res("ad")
        sems = {}
        for sname in ["pe", "ac", "dv", "po", "spq", "sst", "gdq", "scq",
                      "gs1", "gs2", "gs3", "sct"]:
            sems[sname] = ctx.enter_context(nc.semaphore(sname))
        P = Prog(nc)
        emit_setup(P, B)
        for b in range(n_batches):
            emit_load_x(P, B, Xin, b)
            for ph in phases:
                if ph == "moes":
                    emit_ln(P, B, XND=B.XND)
                    emit_moes(P, B, W, Xout, b)
                else:
                    emit_ln(P, B)
                    PHASE_EMIT[ph](P, B, W)
            if "moes" not in phases:
                emit_store_x(P, B, Xout, b)
        P.barrier_wait("sync", [P.now("sst"), P.now("sct"), P.now("gdq")])
        with nc.Block() as block:
            P.replay(block, sems)
    return nc


def _in_maps(inputs, n_cores=8):
    x = np.ascontiguousarray(inputs["L0_attn_x"], dtype=np.float32)
    maps = []
    for c in range(n_cores):
        m = {"x": x[c * NB:(c + 1) * NB]}
        for name, _ in WEIGHT_SPECS:
            m[name] = np.ascontiguousarray(inputs[name], dtype=np.float32)
        maps.append(m)
    return maps


def kernel(**inputs):
    nc = build_program(["l0attn", "l0ffn", "l1attn", "moe"])
    maps = _in_maps(inputs)
    res = run_bass_kernel_spmd(nc, maps, core_ids=list(range(8)))
    return np.concatenate([r["y"] for r in res.results], axis=0)


# version 10
# speedup vs baseline: 1.0655x; 1.0010x over previous
import numpy as np
import concourse.bass as bass
import concourse.mybir as mybir
from concourse.bass import IndirectOffsetOnAxis
from concourse.bass_utils import run_bass_kernel_spmd

F32 = mybir.dt.float32
BF16 = mybir.dt.bfloat16
AF = mybir.ActivationFunctionType
ALU = mybir.AluOpType
AX = mybir.AxisListType.X

D = 1024
S = 2048
NB = 4
H = 16
DK = 64
DC = 256
E = 8
HID = 4096
NT = S // 128
EPS = 1e-5
NEG = -1.0e30
CAP = 768
BIGIX = 1.0e8
I32 = mybir.dt.int32

ENGSEM = {"tensor": "pe", "scalar": "ac", "vector": "dv", "gpsimd": "po",
          "sync": "spq"}


def _mm(o, l, r, st, sp):
    def f(e):
        return e.matmul(o, l, r, start=st, stop=sp)
    return f


def _tr(o, s, idn):
    def f(e):
        return e.transpose(o, s, idn)
    return f


def _cp(o, s):
    def f(e):
        return e.copy(out=o, in_=s)
    return f


def _mulc(o, s, c):
    def f(e):
        return e.mul(out=o, in_=s, mul=c)
    return f


def _act(o, s, func, bias=None, accum=None, scale=None):
    def f(e):
        kw = {}
        if bias is not None:
            kw["bias"] = bias
        if accum is not None:
            kw["accum_out"] = accum
        if scale is not None:
            kw["scale"] = scale
        return e.activation(out=o, in_=s, func=func, **kw)
    return f


def _dma(o, s, transpose=False):
    def f(e):
        if transpose:
            return e.dma_start(out=o, in_=s, transpose=True)
        return e.dma_start(out=o, in_=s)
    return f


def _memset(ap, v):
    def f(e):
        return e.memset(ap, v)
    return f


def _asel(o, cmp, fill, base, pattern, cm):
    def f(e):
        return e.affine_select(out=o, in_=o, compare_op=cmp, fill=fill,
                               base=base, pattern=pattern,
                               channel_multiplier=cm)
    return f


def _rsum(o, s):
    def f(e):
        return e.reduce_sum(o, s, axis=AX)
    return f


def _rmaxn(o, s):
    def f(e):
        return e.reduce_max(o, s, axis=AX, negate=True)
    return f


def _ts(o, i0, s1, s2, op0, op1=None):
    def f(e):
        if op1 is None:
            return e.tensor_scalar(out=o, in0=i0, scalar1=s1, scalar2=None,
                                   op0=op0)
        return e.tensor_scalar(out=o, in0=i0, scalar1=s1, scalar2=s2,
                               op0=op0, op1=op1)
    return f


def _tt(o, i0, i1, op):
    def f(e):
        return e.tensor_tensor(out=o, in0=i0, in1=i1, op=op)
    return f


def _rcpf(o, s):
    def f(e):
        if o.dtype == BF16:
            with e.bass.allow_low_precision(
                    reason="attn denominator reciprocal cast to bf16 for the "
                           "broadcast matmul; ~0.4% scale noise, well inside "
                           "tolerance"):
                return e.reciprocal(o, s)
        return e.reciprocal(o, s)
    return f


def _tcopy(o, s):
    def f(e):
        return e.tensor_copy(out=o, in_=s)
    return f


def _igather(o, src, idx_ap, bound, eoff=0):
    def f(e):
        return e.indirect_dma_start(
            out=o, out_offset=None, in_=src,
            in_offset=IndirectOffsetOnAxis(idx_ap, 0),
            element_offset=eoff, bounds_check=bound, oob_is_err=False)
    return f


def _iscatter(o, idx_ap, src, bound, op=None):
    def f(e):
        return e.indirect_dma_start(
            out=o, out_offset=IndirectOffsetOnAxis(idx_ap, 0), in_=src,
            in_offset=None, bounds_check=bound, oob_is_err=False,
            **({"compute_op": op} if op is not None else {}))
    return f


def _iota(o, pattern, base, cm):
    def f(e):
        return e.iota(o, pattern, base=base, channel_multiplier=cm)
    return f


class Res:
    __slots__ = ("w", "r", "name")

    def __init__(self, name=""):
        self.w = None
        self.r = []
        self.name = name


class Prog:
    def __init__(self, nc):
        self.nc = nc
        self.ops = []
        self.C = {}

    def _bump(self, sem, k):
        self.C[sem] = self.C.get(sem, 0) + k
        return (sem, self.C[sem])

    def op(self, engine, fn, reads=(), writes=(), sem=None, dma=False,
           group=None, last=True, extra=()):
        waits = {}

        def merge(evt):
            if evt is not None:
                s, v = evt
                if waits.get(s, -1) < v:
                    waits[s] = v

        rr = list(reads)
        ww = list(writes)
        for r in rr:
            merge(r.w)
        for w in ww:
            merge(w.w)
            for ev in w.r:
                merge(ev)
        for ev in extra:
            merge(ev)

        semn = sem or ENGSEM[engine]
        k = 16 if dma else 1
        evt = None
        if last:
            evt = self._bump(semn, k)
        self.ops.append((engine, fn, waits, (semn, k) if last else None))

        if group is not None:
            group.extend(rr)
            if last:
                for r in group:
                    r.r.append(evt)
                for w in ww:
                    w.w = evt
                    w.r = []
                del group[:]
        else:
            assert last
            for r in rr:
                r.r.append(evt)
            for w in ww:
                w.w = evt
                w.r = []
        return evt

    def barrier_wait(self, engine, evts):
        waits = {}
        for s, v in evts:
            if waits.get(s, -1) < v:
                waits[s] = v
        self.ops.append((engine, None, waits, None))

    def now(self, sem):
        return (sem, self.C.get(sem, 0))

    def replay(self, block, sems):
        done = {}

        def run(eng_name):
            def body(e):
                for (en, fn, waits, inc) in self.ops:
                    if en != eng_name:
                        continue
                    for s, v in sorted(waits.items()):
                        if done.get((eng_name, s), -1) >= v:
                            continue
                        e.wait_ge(sems[s], v)
                        done[(eng_name, s)] = v
                    if fn is None:
                        continue
                    inst = fn(e)
                    if inc is not None:
                        s, k = inc
                        inst.then_inc(sems[s], k)
            return body

        block.sync(run("sync"))
        block.scalar(run("scalar"))
        block.vector(run("vector"))
        block.tensor(run("tensor"))
        block.gpsimd(run("gpsimd"))


class Bufs:
    pass


def alloc_bufs(nc, ctx):
    B = Bufs()

    def sb(name, shape, dt):
        return ctx.enter_context(nc.sbuf_tensor(name, shape, dt))

    B.XBUF = sb("XBUF", [128, NT * D], F32)
    B.ABUF = sb("ABUF", [128, 8 * S], BF16)
    B.WB1 = sb("WB1", [128, 16384], BF16)
    B.WB2 = sb("WB2", [128, 16 * 1040], BF16)
    B.HB = sb("HB", [128, 8192], BF16)
    B.MB = sb("MB", [128, 8192], BF16)
    B.SMALL = sb("SMALL", [128, 480], F32)
    B.SCR = sb("SCR", [128, 1024], F32)
    B.IDN = sb("IDN", [128, 128], BF16)
    B.MSK = sb("MSK", [128, 2048], BF16)
    B.EPSC = sb("EPSC", [128, 1], F32)
    B.INVD = sb("INVD", [128, 1], F32)
    B.ONES = sb("ONES", [1, 128], BF16)
    B.SUB = sb("SUB", [128, 128], BF16)
    B.ONESF = sb("ONESF", [128, 1], F32)
    B.ONESR = sb("ONESR", [1, 128], F32)
    B.IOTA = sb("IOTA", [128, 16], I32)
    B.IOTAF = sb("IOTAF", [128, 16], F32)
    B.IDXW = sb("IDXW", [128, 8], I32)
    B.IDXS = sb("IDXS", [128, 48], I32)
    B.GG = sb("GG", [128, 48], F32)
    B.CNTB = sb("CNTB", [128, 8], F32)
    B.SMSK = sb("SMSK", [128, 48], F32)
    B.RCP = sb("RCP", [1, 512], BF16)
    B.B2SB = sb("B2SB", [8, 1024], BF16)
    B.RBSB = sb("RBSB", [1, 8], BF16)
    B.PS = ctx.enter_context(nc.psum_tensor("PS", [128, 4096], F32))

    B.r_x = [Res(f"x{t}") for t in range(NT)]
    B.r_xnT = [Res(f"xnT{c}") for c in range(8)]
    B.r_wb1 = Res("wb1")
    B.r_wb2 = Res("wb2")
    B.r_hb = [Res(f"hb{j}") for j in range(4)]
    B.r_mb = [Res(f"mb{j}") for j in range(16)]
    B.r_msk = Res("msk")
    B.r_stat = [Res("stat0"), Res("stat1")]
    B.r_aux = [Res(f"aux{j}") for j in range(8)]
    B.r_scr = Res("scr")
    B.r_idn = Res("idn")
    B.r_eps = Res("eps")
    B.r_sub = Res("sub")
    B.r_onesf = Res("onesf")
    B.r_iota = Res("iota")
    B.r_idxw = Res("idxw")
    B.r_idxs = Res("idxs")
    B.r_gg = Res("gg")
    B.r_cntb = Res("cntb")
    B.r_smsk = Res("smsk")
    B.r_xout = Res("xout")
    B.r_xnd = Res("xnd")
    B.r_gd = Res("gd")
    B.r_idxd = Res("idxd")
    B.r_ones = Res("ones")
    B.r_rcp = Res("rcp")
    B.r_b2 = Res("b2")
    B.r_rb = Res("rb")
    B.r_bank = [Res(f"ps{b}") for b in range(8)]
    return B


def bank(B, i, n=1, part=128, dt=F32):
    ap = B.PS[0:part, i * 512:(i + n) * 512]
    return ap.bitcast(BF16) if dt is BF16 else ap


def xtile(B, t):
    return B.XBUF[:, t * D:(t + 1) * D]


def xnT(B, c):
    return B.ABUF[:, c * S:(c + 1) * S]


def emit_setup(P, B):
    P.op("gpsimd", lambda e: e.reg_mov(B.RB_TOK, S - 1), writes=[])
    P.op("gpsimd", lambda e: e.reg_mov(B.RB_IDX, 8 * CAP - 1), writes=[])
    P.op("gpsimd", _memset(B.IDN[:], 0.0), writes=[B.r_idn])
    P.op("gpsimd", _asel(B.IDN[:], ALU.not_equal, 1.0, 0, [[-1, 128]], 1),
         reads=[B.r_idn], writes=[B.r_idn])
    P.op("gpsimd", _memset(B.ONES[:], 1.0), writes=[B.r_ones])
    P.op("gpsimd", _memset(B.EPSC[:], EPS), writes=[B.r_eps])
    P.op("gpsimd", _memset(B.INVD[:], 1.0 / D), writes=[B.r_eps])
    P.op("gpsimd", _memset(B.MSK[:], 1.0), writes=[B.r_msk])
    P.op("gpsimd", _memset(B.SUB[:], 0.0), writes=[B.r_sub])
    P.op("gpsimd", _asel(B.SUB[:], ALU.is_ge, 1.0, 0, [[-1, 128]], 1),
         reads=[B.r_sub], writes=[B.r_sub])
    P.op("gpsimd", _memset(B.ONESF[:], 1.0), writes=[B.r_onesf])
    P.op("gpsimd", _memset(B.ONESR[:], 1.0), writes=[B.r_onesf])
    P.op("gpsimd", _iota(B.IOTA[:], [[128, 16]], 0, 1), writes=[B.r_iota])
    P.op("vector", _tcopy(B.IOTAF[:], B.IOTA[:]),
         reads=[B.r_iota], writes=[B.r_iota])
    for j in range(4):
        P.op("gpsimd", _memset(B.HB[:, j * 2048:(j + 1) * 2048], 0.0),
             writes=[B.r_hb[j]])
    for j in range(4):
        P.op("gpsimd", _asel(B.MSK[:, j * 512:(j + 1) * 512], ALU.is_ge, 0.0,
                             -j * 128, [[1, 512]], -1),
             reads=[B.r_msk], writes=[B.r_msk])


def emit_load_x(P, B, Xin, b):
    src = Xin[b].rearrange("(t p) d -> p t d", p=128)
    dst = B.XBUF[:].rearrange("p (t d) -> p t d", d=D)
    P.op("sync", _dma(dst, src), writes=B.r_x, dma=True, sem="spq")


def emit_store_x(P, B, Xout, b):
    dst = Xout[b].rearrange("(t p) d -> p t d", p=128)
    src = B.XBUF[:].rearrange("p (t d) -> p t d", d=D)
    P.op("sync", _dma(dst, src), reads=B.r_x, dma=True, sem="sst")


def emit_ln(P, B, XND=None):
    st = B.SMALL
    for g in range(2):
        for i in range(8):
            t = g * 8 + i
            xt = xtile(B, t)
            xtok = B.MB[:, i * 1024:(i + 1) * 1024]
            r_xtok = [B.r_mb[2 * i], B.r_mb[2 * i + 1]]
            par = t % 2
            r_st = B.r_stat[par]
            o = par * 8
            c_sum, c_ssq = st[:, o:o + 1], st[:, o + 1:o + 2]
            c_var = st[:, o + 3:o + 4]
            c_rstd, c_nmr = st[:, o + 4:o + 5], st[:, o + 5:o + 6]
            P.op("vector", _rsum(c_sum, xt), reads=[B.r_x[t], r_st],
                 writes=[r_st])
            P.op("scalar", _act(B.SCR[:], xt, AF.Square, accum=c_ssq),
                 reads=[B.r_x[t], r_st], writes=[B.r_scr, r_st])
            P.op("vector", _ts(c_var, c_sum, c_sum, None, ALU.mult),
                 reads=[r_st], writes=[r_st])
            P.op("vector", _ts(c_var, c_var, -1.0 / D, c_ssq, ALU.mult,
                               ALU.add), reads=[r_st], writes=[r_st])
            P.op("scalar", _act(c_rstd, c_var, AF.Sqrt, bias=B.EPSC[:],
                                scale=B.INVD[:]),
                 reads=[r_st, B.r_eps], writes=[r_st])
            P.op("vector", _rcpf(c_rstd, c_rstd), reads=[r_st], writes=[r_st])
            P.op("vector", _ts(c_nmr, c_sum, c_rstd, -1.0 / D, ALU.mult,
                               ALU.mult), reads=[r_st], writes=[r_st])
            P.op("vector", _ts(xtok, xt, c_rstd, c_nmr, ALU.mult, ALU.add),
                 reads=[B.r_x[t], r_st], writes=r_xtok)
        for c in range(8):
            bks = [B.r_bank[c % 2]]
            pb = bank(B, c % 2, 1, dt=BF16)
            grp = []
            for i in range(8):
                P.op("tensor",
                     _tr(pb[:, i * 128:(i + 1) * 128],
                         B.MB[:, i * 1024 + c * 128:i * 1024 + c * 128 + 128],
                         B.IDN[:]),
                     reads=[B.r_mb[2 * i], B.r_mb[2 * i + 1], B.r_idn],
                     writes=bks if i in (0, 7) else [],
                     group=grp, last=(i == 7))
            P.op("scalar", _cp(xnT(B, c)[:, g * 1024:(g + 1) * 1024], pb),
                 reads=bks, writes=[B.r_xnT[c]])
        if XND is not None:
            P.op("sync", _dma(
                XND[g * 1024:(g + 1) * 1024].rearrange("(t p) d -> p t d",
                                                       p=128),
                B.MB[:].rearrange("p (t d) -> p t d", d=1024)),
                reads=[B.r_mb[j] for j in range(16)], writes=[B.r_xnd],
                dma=True, sem="spq")


def cast_load(P, B, dst_ap, src_ap, dst_res):
    P.op("gpsimd", _dma(dst_ap, src_ap), writes=[dst_res], dma=True,
         sem="gdq")


def emit_l0attn(P, B, W):
    wdq = W["L0_attn_W_dq"].rearrange("(c p) k -> c p k", p=128)
    wdkv = W["L0_attn_W_dkv"].rearrange("(c p) k -> c p k", p=128)
    wuq = W["L0_attn_W_uq"].rearrange("(c p) k -> c p k", p=128)
    wuk = W["L0_attn_W_uk"].rearrange("(c p) k -> c p k", p=128)
    wuv = W["L0_attn_W_uv"].rearrange("(c p) k -> c p k", p=128)
    wo = W["L0_attn_W_o"].rearrange("(c p) k -> c p k", p=128)
    for c in range(8):
        cast_load(P, B, B.WB1[:, c * 256:(c + 1) * 256], wdq[c], B.r_wb1)
        cast_load(P, B, B.WB1[:, 2048 + c * 256:2048 + (c + 1) * 256],
                  wdkv[c], B.r_wb1)
    for cc in range(2):
        cast_load(P, B, B.WB1[:, 4096 + cc * 1024:4096 + (cc + 1) * 1024],
                  wuq[cc], B.r_wb1)
        cast_load(P, B, B.WB1[:, 6144 + cc * 1024:6144 + (cc + 1) * 1024],
                  wuk[cc], B.r_wb1)
        cast_load(P, B, B.WB1[:, 8192 + cc * 1024:8192 + (cc + 1) * 1024],
                  wuv[cc], B.r_wb1)

    pp = 0
    for base_w, base_hb in ((0, 0), (2048, 4096)):
        for cc in range(2):
            for tc in range(4):
                bk = 4 + (pp % 2)
                pp += 1
                pb = bank(B, bk)
                grp = []
                for c in range(8):
                    wcol = base_w + c * 256 + cc * 128
                    P.op("tensor",
                         _mm(pb, B.WB1[:, wcol:wcol + 128],
                             xnT(B, c)[:, tc * 512:(tc + 1) * 512],
                             c == 0, c == 7),
                         reads=[B.r_wb1, B.r_xnT[c]],
                         writes=[B.r_bank[bk]] if c in (0, 7) else [],
                         group=grp, last=(c == 7))
                hcol = base_hb + cc * 2048 + tc * 512
                P.op("scalar", _cp(B.HB[:, hcol:hcol + 512], pb),
                     reads=[B.r_bank[bk]], writes=[B.r_hb[hcol // 2048]])

    for t in range(NT):
        P.op("gpsimd", _memset(
            B.WB2[:, t * 1040:(t + 1) * 1040].rearrange(
                "p (h f) -> p h f", f=65)[:, :, 64:65], 1.0),
            writes=[B.r_wb2])
    for t in range(NT):
        b0 = 4 + (t % 2) * 2
        pb = bank(B, b0, 2)
        bks = [B.r_bank[b0], B.r_bank[b0 + 1]]
        grp = []
        for cc in range(2):
            for dh in range(2):
                lcol = 4096 + cc * 2048 + t * 128
                rcol = 8192 + cc * 1024 + dh * 512
                P.op("tensor",
                     _mm(pb[:, dh * 512:(dh + 1) * 512],
                         B.HB[:, lcol:lcol + 128],
                         B.WB1[:, rcol:rcol + 512], cc == 0, cc == 1),
                     reads=[B.r_hb[2 + cc], B.r_wb1],
                     writes=bks if (cc, dh) in ((0, 0), (1, 1)) else [],
                     group=grp, last=(cc == 1 and dh == 1))
        P.op("vector", _tcopy(
            B.WB2[:, t * 1040:(t + 1) * 1040].rearrange(
                "p (h f) -> p h f", f=65)[:, :, 0:64],
            pb.rearrange("p (h f) -> p h f", f=64)),
            reads=bks, writes=[B.r_wb2])

    qt_slots = [B.r_mb[j] for j in range(4)]
    kt_slots = [B.r_mb[4 + j] for j in range(4)]
    pending_tail = None
    for p in range(8):
        for wbase, hb_base, mbase, slots, scale in (
                (4096, 0, 0, qt_slots, 0.125),
                (6144, 4096, 2048, kt_slots, None)):
            for tc in range(4):
                bk = 4 + (tc % 2)
                pb = bank(B, bk)
                grp = []
                for cc in range(2):
                    lcol = wbase + cc * 1024 + p * 128
                    rcol = hb_base + cc * 2048 + tc * 512
                    P.op("tensor",
                         _mm(pb, B.WB1[:, lcol:lcol + 128],
                             B.HB[:, rcol:rcol + 512], cc == 0, cc == 1),
                         reads=[B.r_wb1, B.r_hb[(hb_base + cc * 2048) // 2048]],
                         writes=[B.r_bank[bk]],
                         group=grp, last=(cc == 1))
                mcol = mbase + tc * 512
                if scale is None:
                    P.op("scalar", _cp(B.MB[:, mcol:mcol + 512], pb),
                         reads=[B.r_bank[bk]], writes=[slots[tc]])
                else:
                    P.op("scalar", _mulc(B.MB[:, mcol:mcol + 512], pb, scale),
                         reads=[B.r_bank[bk]], writes=[slots[tc]])
        if p == 7:
            for c in range(8):
                cast_load(P, B, B.WB1[:, c * 1024:(c + 1) * 1024], wo[c],
                          B.r_wb1)
        for hh in range(2):
            h = 2 * p + hh
            rs = hh * 64
            for qc in range(4):
                kn = 4 * (qc + 1)
                avb = 4 + ((h * 4 + qc) % 2)
                av = bank(B, avb, part=65)
                grp_av = []
                npr = kn // 2

                def emit_front(pr):
                    sbk = (pr % 2) * 2
                    spb = bank(B, sbk, 2)
                    pcol = (4096, 5120, 6656)[pr % 3]
                    pT2 = B.MB[:, pcol:pcol + 1024]
                    r_pT = [B.r_mb[pcol // 512], B.r_mb[pcol // 512 + 1]]
                    sgrp = []
                    for j in range(2):
                        kc = 2 * pr + j
                        P.op("tensor",
                             _mm(spb[:, j * 512:(j + 1) * 512],
                                 B.MB[rs:rs + 64,
                                      2048 + kc * 128:2048 + kc * 128 + 128],
                                 B.MB[rs:rs + 64, qc * 512:qc * 512 + 512],
                                 True, True),
                             reads=qt_slots + kt_slots,
                             writes=[B.r_bank[sbk], B.r_bank[sbk + 1]],
                             group=sgrp, last=(j == 1))
                    P.op("scalar", _act(pT2, spb, AF.Exp),
                         reads=[B.r_bank[sbk], B.r_bank[sbk + 1]] + r_pT,
                         writes=r_pT)
                    for j in range(2):
                        kc = 2 * pr + j
                        if kc >= qc * 4:
                            jj = kc - qc * 4
                            P.op("vector",
                                 _tt(pT2[:, j * 512:(j + 1) * 512],
                                     pT2[:, j * 512:(j + 1) * 512],
                                     B.MSK[:, jj * 512:jj * 512 + 512],
                                     ALU.mult),
                                 reads=r_pT + [B.r_msk], writes=r_pT)

                def emit_av(pr):
                    pcol = (4096, 5120, 6656)[pr % 3]
                    pT2 = B.MB[:, pcol:pcol + 1024]
                    r_pT = [B.r_mb[pcol // 512], B.r_mb[pcol // 512 + 1]]
                    for j in range(2):
                        kc = 2 * pr + j
                        vcol = kc * 1040 + h * 65
                        P.op("tensor",
                             _mm(av, B.WB2[:, vcol:vcol + 65],
                                 pT2[:, j * 512:(j + 1) * 512],
                                 kc == 0, kc == kn - 1),
                             reads=[B.r_wb2] + r_pT,
                             writes=[B.r_bank[avb]]
                             if kc in (0, kn - 1) else [],
                             group=grp_av, last=(j == 1))

                for pr in range(npr + 1):
                    if pr < npr:
                        emit_front(pr)
                    if pr >= 1:
                        emit_av(pr - 1)

                def tail(h=h, qc=qc, avb=avb, av=av, rs=rs):
                    P.op("vector", _rcpf(B.RCP[:], av[64:65, :]),
                         reads=[B.r_bank[avb], B.r_rcp], writes=[B.r_rcp])
                    bcb = 6 + (qc % 2)
                    bc = bank(B, bcb, part=64)
                    P.op("tensor", _mm(bc, B.ONES[:, 0:64], B.RCP[:], True,
                                       True),
                         reads=[B.r_ones, B.r_rcp], writes=[B.r_bank[bcb]])
                    P.op("scalar", _cp(B.SCR[0:64, 0:512], bc),
                         reads=[B.r_bank[bcb], B.r_scr], writes=[B.r_scr])
                    P.op("vector", _tt(
                        xnT(B, h // 2)[rs:rs + 64, qc * 512:qc * 512 + 512],
                        av[0:64, :], B.SCR[0:64, 0:512], ALU.mult),
                        reads=[B.r_bank[avb], B.r_scr],
                        writes=[B.r_xnT[h // 2]])

                tail()

    if pending_tail is not None:
        pending_tail()

    for t in range(NT):
        b0 = 4 + (t % 2) * 2
        pb = bank(B, b0, 2)
        bks = [B.r_bank[b0], B.r_bank[b0 + 1]]
        grp = []
        for c in range(8):
            for dh in range(2):
                rcol = c * 1024 + dh * 512
                P.op("tensor",
                     _mm(pb[:, dh * 512:(dh + 1) * 512],
                         xnT(B, c)[:, t * 128:(t + 1) * 128],
                         B.WB1[:, rcol:rcol + 512], c == 0, c == 7),
                     reads=[B.r_xnT[c], B.r_wb1],
                     writes=bks if (c, dh) in ((0, 0), (7, 1)) else [],
                     group=grp, last=(c == 7 and dh == 1))
        P.op("vector", _tt(xtile(B, t), xtile(B, t), pb, ALU.add),
             reads=bks + [B.r_x[t]], writes=[B.r_x[t]])


def emit_l0ffn(P, B, W):
    Wap = W["L0_ffn_weight"].rearrange("(c p) i -> c p i", p=128)
    for c in range(8):
        cast_load(P, B, B.HB[:, c * 1024:(c + 1) * 1024], Wap[c],
                  B.r_hb[c // 2])
    P.op("scalar", _act(B.HB[:], B.HB[:], AF.Sign),
         reads=B.r_hb, writes=B.r_hb)
    for c in range(8):
        bks = [B.r_bank[c % 2]]
        pb = bank(B, c % 2, 1, dt=BF16)
        grp = []
        for oc in range(8):
            scol = oc * 1024 + c * 128
            P.op("tensor",
                 _tr(pb[:, oc * 128:(oc + 1) * 128],
                     B.HB[:, scol:scol + 128], B.IDN[:]),
                 reads=[B.r_hb[oc // 2], B.r_idn],
                 writes=bks if oc in (0, 7) else [],
                 group=grp, last=(oc == 7))
        P.op("scalar", _cp(B.WB1[:, c * 1024:(c + 1) * 1024], pb),
             reads=bks, writes=[B.r_wb1])
    for t in range(NT):
        b0 = 4 + (t % 2) * 2
        pb = bank(B, b0, 2)
        bks = [B.r_bank[b0], B.r_bank[b0 + 1]]
        grp = []
        for c in range(8):
            for dh in range(2):
                rcol = c * 1024 + dh * 512
                P.op("tensor",
                     _mm(pb[:, dh * 512:(dh + 1) * 512],
                         xnT(B, c)[:, t * 128:(t + 1) * 128],
                         B.WB1[:, rcol:rcol + 512], c == 0, c == 7),
                     reads=[B.r_xnT[c], B.r_wb1],
                     writes=bks if (c, dh) in ((0, 0), (7, 1)) else [],
                     group=grp, last=(c == 7 and dh == 1))
        P.op("vector", _tt(xtile(B, t), xtile(B, t), pb, ALU.add),
             reads=bks + [B.r_x[t]], writes=[B.r_x[t]])


def emit_l1attn(P, B, W):
    wq = W["L1_attn_Wq"].rearrange("h (c p) k -> c p h k", p=128)
    wk = W["L1_attn_Wk"].rearrange("h (c p) k -> c p h k", p=128)
    wv = W["L1_attn_Wv"].rearrange("h (c p) k -> c p h k", p=128)

    def wb1_hk(col):
        return B.WB1[:, col:col + 1024].rearrange("p (h k) -> p h k", k=64)

    for c in range(8):
        cast_load(P, B, wb1_hk(c * 1024), wv[c], B.r_wb1)
    for t in range(NT):
        b0 = 4 + (t % 2) * 2
        pb = bank(B, b0, 2)
        bks = [B.r_bank[b0], B.r_bank[b0 + 1]]
        grp = []
        for c in range(8):
            for dh in range(2):
                rcol = c * 1024 + dh * 512
                P.op("tensor",
                     _mm(pb[:, dh * 512:(dh + 1) * 512],
                         xnT(B, c)[:, t * 128:(t + 1) * 128],
                         B.WB1[:, rcol:rcol + 512], c == 0, c == 7),
                     reads=[B.r_xnT[c], B.r_wb1],
                     writes=bks if (c, dh) in ((0, 0), (7, 1)) else [],
                     group=grp, last=(c == 7 and dh == 1))
        P.op("scalar", _mulc(B.WB2[:, t * 1024:(t + 1) * 1024], pb, 0.125),
             reads=bks, writes=[B.r_wb2])
    for c in range(8):
        cast_load(P, B, wb1_hk(c * 1024), wq[c], B.r_wb1)
        cast_load(P, B, wb1_hk(8192 + c * 1024), wk[c], B.r_wb1)

    p_bufs = [
        (B.MB[:, 0:2048], [B.r_mb[j] for j in range(4)]),
        (B.MB[:, 2048:4096], [B.r_mb[4 + j] for j in range(4)]),
        (B.HB[:, 4096:6144], [B.r_hb[2]]),
        (B.HB[:, 6144:8192], [B.r_hb[3]]),
    ]

    for p in range(8):
        for wbase, hbase, bki in ((0, 0, 0), (8192, 2048, 2)):
            for tc in range(4):
                bk = bki + (tc % 2)
                pb = bank(B, bk)
                grp = []
                for c in range(8):
                    lcol = wbase + c * 1024 + p * 128
                    P.op("tensor",
                         _mm(pb, B.WB1[:, lcol:lcol + 128],
                             xnT(B, c)[:, tc * 512:(tc + 1) * 512],
                             c == 0, c == 7),
                         reads=[B.r_wb1, B.r_xnT[c]],
                         writes=[B.r_bank[bk]] if c in (0, 7) else [],
                         group=grp, last=(c == 7))
                hcol = hbase + tc * 512
                P.op("scalar", _cp(B.HB[:, hcol:hcol + 512], pb),
                     reads=[B.r_bank[bk]], writes=[B.r_hb[hbase // 2048]])
        for hh in range(2):
            h = 2 * p + hh
            rs = hh * 64
            for qt in range(NT):
                par = qt % 2
                r_st = B.r_aux[par]
                o = par * 8
                stc = B.SMALL[:, 16 + o:16 + o + 8]
                nm = stc[:, 2:3]
                dS = stc[:, 5:6]
                sset = bank(B, par * 4, 4)
                sres = [B.r_bank[par * 4 + j] for j in range(4)]
                grp = []
                for kc in range(4):
                    P.op("tensor",
                         _mm(sset[:, kc * 512:(kc + 1) * 512],
                             B.HB[rs:rs + 64, qt * 128:qt * 128 + 128],
                             B.HB[rs:rs + 64,
                                  2048 + kc * 512:2048 + kc * 512 + 512],
                             True, True),
                         reads=[B.r_hb[0], B.r_hb[1]],
                         writes=sres,
                         group=grp, last=(kc == 3))
                P.op("vector", _rmaxn(nm, sset),
                     reads=sres + [r_st], writes=[r_st])
                pi = (h * NT + qt) % 4
                psb, r_p = p_bufs[pi]
                P.op("scalar", _act(psb, sset, AF.Exp, bias=nm, accum=dS),
                     reads=sres + [r_st] + r_p,
                     writes=r_p + [r_st])
                rc = B.SMALL[:, 32 + qt * 16 + h:32 + qt * 16 + h + 1]
                P.op("vector", _rcpf(rc, dS),
                     reads=[r_st, B.r_aux[2]], writes=[B.r_aux[2]])
                P.op("sync", _dma(B.PD[h][qt * 128:qt * 128 + 128, :], psb),
                     reads=r_p, writes=[B.r_pd[h]], dma=True, sem="spq")

    pt_bufs = [
        (B.MB[:, 4096:6144], [B.r_mb[8 + j] for j in range(4)]),
        (B.MB[:, 6144:8192], [B.r_mb[12 + j] for j in range(4)]),
    ]
    for h in range(H):
        avb = (h % 2) * 4
        av = B.PS[0:64, avb * 512:(avb + 4) * 512]
        avres = [B.r_bank[avb + j] for j in range(4)]
        grp_av = []
        for kc in range(NT):
            ptb, r_pt = pt_bufs[kc % 2]
            P.op("sync" if kc % 2 == 0 else "scalar",
                 _dma(ptb, B.PD[h][:, kc * 128:kc * 128 + 128],
                      transpose=True),
                 reads=[B.r_pd[h]] + r_pt, writes=r_pt, dma=True,
                 sem="spq" if kc % 2 == 0 else "scq")
            for qc in range(4):
                vcol = kc * 1024 + h * 64
                P.op("tensor",
                     _mm(av[:, qc * 512:(qc + 1) * 512],
                         B.WB2[:, vcol:vcol + 64],
                         ptb[:, qc * 512:(qc + 1) * 512],
                         kc == 0, kc == NT - 1),
                     reads=[B.r_wb2] + r_pt,
                     writes=avres
                     if (kc == 0 and qc == 0) or (kc == NT - 1 and qc == 3)
                     else [],
                     group=grp_av, last=(qc == 3))
        avst = B.SCR[0:64, :].bitcast(BF16)
        P.op("scalar", _cp(avst, av),
             reads=avres + [B.r_scr], writes=[B.r_scr])
        P.op("sync", _dma(B.AD[h * 64:h * 64 + 64, :], avst),
             reads=[B.r_scr], writes=[B.r_ad], dma=True, sem="spq")

    for t in range(NT):
        acol = (t % 2) * 1024
        at = B.MB[:, acol:acol + 1024]
        r_at = [B.r_mb[(t % 2) * 2], B.r_mb[(t % 2) * 2 + 1]]
        P.op("scalar", _dma(at, B.AD[:, t * 128:t * 128 + 128],
                            transpose=True),
             reads=[B.r_ad] + r_at, writes=r_at, dma=True, sem="scq")
        for h in range(H):
            rc = B.SMALL[:, 32 + t * 16 + h:32 + t * 16 + h + 1]
            P.op("vector", _ts(B.SCR[:, h * 64:h * 64 + 64],
                               at[:, h * 64:h * 64 + 64], rc, None, ALU.mult),
                 reads=r_at + [B.r_aux[2]], writes=[B.r_scr])
        P.op("vector", _tt(xtile(B, t), xtile(B, t), B.SCR[:], ALU.add),
             reads=[B.r_scr, B.r_x[t]], writes=[B.r_x[t]])


def emit_moe(P, B, W):
    rw = W["L1_ffn_router_w"].rearrange("(c p) e -> p c e", p=128)
    cast_load(P, B,
              B.MB[:, 6400:6464].rearrange("p (c e) -> p c e", e=E),
              rw, B.r_mb[12])
    P.op("gpsimd", _dma(B.RBSB[:], W["L1_ffn_router_b"].unsqueeze(0)),
         writes=[B.r_rb], dma=True, sem="gs2")
    P.op("gpsimd", _dma(B.B2SB[:], W["L1_ffn_b2"]),
         writes=[B.r_b2], dma=True, sem="gs1")

    st = B.SMALL
    for t in range(NT):
        lg = B.PS[:, 6 * 512:6 * 512 + 8]
        grp = []
        for c in range(8):
            P.op("tensor",
                 _mm(lg, xnT(B, c)[:, t * 128:(t + 1) * 128],
                     B.MB[:, 6400 + c * 8:6400 + c * 8 + 8], c == 0, False),
                 reads=[B.r_xnT[c], B.r_mb[12]],
                 writes=[B.r_bank[6]] if c == 0 else [],
                 group=grp, last=False)
        P.op("tensor", _mm(lg, B.ONES[:, 0:128], B.RBSB[:], False, True),
             reads=[B.r_ones, B.r_rb], writes=[B.r_bank[6]], group=grp,
             last=True)
        nm1, nm2 = st[:, 448:449], st[:, 449:450]
        den, rcp = st[:, 450:451], st[:, 451:452]
        m1 = B.SCR[:, 0:8]
        l2 = B.SCR[:, 8:16]
        e1 = B.SCR[:, 16:24]
        kp = B.SCR[:, 24:32]
        r_g = B.r_aux[5]
        P.op("vector", _rmaxn(nm1, lg),
             reads=[B.r_bank[6], r_g], writes=[r_g])
        P.op("vector", _ts(m1, lg, nm1, 0.0, ALU.add, ALU.is_ge),
             reads=[B.r_bank[6], r_g, B.r_scr], writes=[B.r_scr])
        P.op("vector", _ts(l2, m1, NEG, None, ALU.mult),
             reads=[B.r_scr], writes=[B.r_scr])
        P.op("vector", _tt(l2, l2, lg, ALU.add),
             reads=[B.r_scr, B.r_bank[6]], writes=[B.r_scr])
        P.op("vector", _rmaxn(nm2, l2),
             reads=[B.r_scr, r_g], writes=[r_g])
        P.op("scalar", _act(e1, lg, AF.Exp, bias=nm1),
             reads=[B.r_bank[6], r_g, B.r_scr], writes=[B.r_scr])
        P.op("vector", _ts(kp, lg, nm2, 0.0, ALU.add, ALU.is_ge),
             reads=[B.r_bank[6], r_g, B.r_scr], writes=[B.r_scr])
        P.op("vector", _tt(e1, e1, kp, ALU.mult),
             reads=[B.r_scr], writes=[B.r_scr])
        P.op("vector", _rsum(den, e1), reads=[B.r_scr, r_g], writes=[r_g])
        P.op("vector", _rcpf(rcp, den), reads=[r_g], writes=[r_g])
        P.op("vector", _ts(st[:, 288 + t * 8:288 + t * 8 + 8], e1, rcp, None,
                           ALU.mult),
             reads=[B.r_scr, r_g, B.r_aux[3]], writes=[B.r_aux[3]])
        P.op("vector", _tcopy(B.MB[:, 6144:6152],
                              st[:, 288 + t * 8:288 + t * 8 + 8]),
             reads=[B.r_aux[3], B.r_mb[12]], writes=[B.r_mb[12]])
        gt = bank(B, 7, dt=BF16)
        P.op("tensor", _tr(gt[0:8, 0:128], B.MB[:, 6144:6152], B.IDN[:]),
             reads=[B.r_mb[12], B.r_idn], writes=[B.r_bank[7]])
        P.op("scalar", _cp(B.MB[0:8, 6272:6400], gt[0:8, 0:128]),
             reads=[B.r_bank[7]], writes=[B.r_mb[12]])
        pb = bank(B, 2, 2)
        grpb = []
        for dh in range(2):
            P.op("tensor",
                 _mm(pb[:, dh * 512:(dh + 1) * 512], B.MB[0:8, 6272:6400],
                     B.B2SB[0:8, dh * 512:dh * 512 + 512], True, True),
                 reads=[B.r_mb[12], B.r_b2],
                 writes=[B.r_bank[2], B.r_bank[3]],
                 group=grpb, last=(dh == 1))
        P.op("vector", _tt(xtile(B, t), xtile(B, t), pb, ALU.add),
             reads=[B.r_bank[2], B.r_bank[3], B.r_x[t]], writes=[B.r_x[t]])

    w1 = W["L1_ffn_w1"]
    w2 = W["L1_ffn_w2"]
    b1 = W["L1_ffn_b1"]
    for ei in range(E):
        P.op("gpsimd", _dma(B.MB[0:32, 6464:6592],
                            b1[ei].rearrange("(c p) -> c p", p=128)),
             reads=[B.r_mb[12]], writes=[B.r_mb[12]], dma=True, sem="gs3")
        b1t = bank(B, 7, dt=BF16)
        P.op("tensor", _tr(b1t[0:128, 0:32], B.MB[0:32, 6464:6592],
                           B.IDN[0:32, 0:32]),
             reads=[B.r_mb[12], B.r_idn], writes=[B.r_bank[7]])
        P.op("scalar", _cp(st[:, 416:448], b1t[0:128, 0:32]),
             reads=[B.r_bank[7], B.r_aux[4]], writes=[B.r_aux[4]])
        for half in range(2):
            w1s = w1[ei][:, half * 2048:(half + 1) * 2048].rearrange(
                "(c p) h -> p c h", p=128)
            w2s = w2[ei][half * 2048:(half + 1) * 2048, :].rearrange(
                "(c p) d -> p c d", p=128)
            cast_load(P, B, B.WB1[:].rearrange("p (c h) -> p c h", c=8),
                      w1s, B.r_wb1)
            cast_load(P, B,
                      B.WB2[:, 0:16384].rearrange("p (c d) -> p c d", c=16),
                      w2s, B.r_wb2)
            for tc in range(4):
                for hc in range(16):
                    bk = (0, 1, 6, 7)[hc % 4]
                    pb = bank(B, bk)
                    grp = []
                    for c in range(8):
                        lcol = c * 2048 + hc * 128
                        P.op("tensor",
                             _mm(pb, B.WB1[:, lcol:lcol + 128],
                                 xnT(B, c)[:, tc * 512:(tc + 1) * 512],
                                 c == 0, c == 7),
                             reads=[B.r_wb1, B.r_xnT[c]],
                             writes=[B.r_bank[bk]] if c in (0, 7) else [],
                             group=grp, last=(c == 7))
                    bia = st[:, 416 + half * 16 + hc:417 + half * 16 + hc]
                    P.op("scalar", _act(B.HB[:, hc * 512:hc * 512 + 512], pb,
                                        AF.Relu, bias=bia),
                         reads=[B.r_bank[bk], B.r_aux[4]],
                         writes=[B.r_hb[hc // 4]])
                for tt in range(4):
                    t = tc * 4 + tt
                    b0 = 2 + (tt % 2) * 2
                    pb = bank(B, b0, 2)
                    bks = [B.r_bank[b0], B.r_bank[b0 + 1]]
                    grp = []
                    for hc in range(16):
                        for dh in range(2):
                            lcol = hc * 512 + tt * 128
                            rcol = hc * 1024 + dh * 512
                            P.op("tensor",
                                 _mm(pb[:, dh * 512:(dh + 1) * 512],
                                     B.HB[:, lcol:lcol + 128],
                                     B.WB2[:, rcol:rcol + 512],
                                     hc == 0, hc == 15),
                                 reads=[B.r_hb[hc // 4], B.r_wb2],
                                 writes=bks
                                 if (hc, dh) in ((0, 0), (15, 1)) else [],
                                 group=grp, last=(hc == 15 and dh == 1))
                    gcol = st[:, 288 + t * 8 + ei:288 + t * 8 + ei + 1]
                    P.op("vector", _ts(B.SCR[:], pb, gcol, None, ALU.mult),
                         reads=bks + [B.r_aux[3], B.r_scr],
                         writes=[B.r_scr])
                    P.op("vector", _tt(xtile(B, t), xtile(B, t), B.SCR[:],
                                       ALU.add),
                         reads=[B.r_scr, B.r_x[t]], writes=[B.r_x[t]])



def emit_moes(P, B, W, Xout, b):
    st = B.SMALL
    XOUTB = Xout[b]
    rw = W["L1_ffn_router_w"].rearrange("(c p) e -> p c e", p=128)
    cast_load(P, B,
              B.MB[:, 6400:6464].rearrange("p (c e) -> p c e", e=E),
              rw, B.r_mb[12])
    P.op("gpsimd", _dma(B.RBSB[:], W["L1_ffn_router_b"].unsqueeze(0)),
         writes=[B.r_rb], dma=True, sem="gs2")
    P.op("gpsimd", _dma(B.B2SB[:], W["L1_ffn_b2"]),
         writes=[B.r_b2], dma=True, sem="gs1")
    base = st[0:1, 452:460]
    cnt_s = st[0:1, 460:468]
    r_base = B.r_aux[6]
    P.op("gpsimd", _memset(base, 0.0), writes=[r_base])

    for t in range(NT):
        lg = B.PS[:, 6 * 512:6 * 512 + 8]
        grp = []
        for c in range(8):
            P.op("tensor",
                 _mm(lg, xnT(B, c)[:, t * 128:(t + 1) * 128],
                     B.MB[:, 6400 + c * 8:6400 + c * 8 + 8], c == 0, False),
                 reads=[B.r_xnT[c], B.r_mb[12]],
                 writes=[B.r_bank[6]] if c == 0 else [],
                 group=grp, last=False)
        P.op("tensor", _mm(lg, B.ONES[:, 0:128], B.RBSB[:], False, True),
             reads=[B.r_ones, B.r_rb], writes=[B.r_bank[6]], group=grp,
             last=True)
        nm1, nm2 = st[:, 448:449], st[:, 449:450]
        den, rcp = st[:, 450:451], st[:, 451:452]
        m1 = B.SCR[:, 0:8]
        l2 = B.SCR[:, 8:16]
        e1 = B.SCR[:, 16:24]
        kp = B.SCR[:, 24:32]
        nk = B.SCR[:, 32:40]
        posm = B.SCR[:, 40:48]
        r_g = B.r_aux[5]
        P.op("vector", _rmaxn(nm1, lg),
             reads=[B.r_bank[6], r_g], writes=[r_g])
        P.op("vector", _ts(m1, lg, nm1, 0.0, ALU.add, ALU.is_ge),
             reads=[B.r_bank[6], r_g, B.r_scr], writes=[B.r_scr])
        P.op("vector", _ts(l2, m1, NEG, None, ALU.mult),
             reads=[B.r_scr], writes=[B.r_scr])
        P.op("vector", _tt(l2, l2, lg, ALU.add),
             reads=[B.r_scr, B.r_bank[6]], writes=[B.r_scr])
        P.op("vector", _rmaxn(nm2, l2),
             reads=[B.r_scr, r_g], writes=[r_g])
        P.op("scalar", _act(e1, lg, AF.Exp, bias=nm1),
             reads=[B.r_bank[6], r_g, B.r_scr], writes=[B.r_scr])
        P.op("vector", _ts(kp, lg, nm2, 0.0, ALU.add, ALU.is_ge),
             reads=[B.r_bank[6], r_g, B.r_scr], writes=[B.r_scr])
        P.op("vector", _tt(e1, e1, kp, ALU.mult),
             reads=[B.r_scr], writes=[B.r_scr])
        P.op("vector", _rsum(den, e1), reads=[B.r_scr, r_g], writes=[r_g])
        P.op("vector", _rcpf(rcp, den), reads=[r_g], writes=[r_g])
        P.op("vector", _ts(st[:, 288 + t * 8:288 + t * 8 + 8], e1, rcp, None,
                           ALU.mult),
             reads=[B.r_scr, r_g, B.r_aux[3]], writes=[B.r_aux[3]])
        P.op("sync", _dma(B.GD[t * 128:(t + 1) * 128, :],
                          st[:, 288 + t * 8:288 + t * 8 + 8]),
             reads=[B.r_aux[3]], writes=[B.r_gd], dma=True, sem="spq")
        P.op("vector", _tcopy(B.MB[:, 6144:6152],
                              st[:, 288 + t * 8:288 + t * 8 + 8]),
             reads=[B.r_aux[3], B.r_mb[12]], writes=[B.r_mb[12]])
        gt = bank(B, 7, dt=BF16)
        P.op("tensor", _tr(gt[0:8, 0:128], B.MB[:, 6144:6152], B.IDN[:]),
             reads=[B.r_mb[12], B.r_idn], writes=[B.r_bank[7]])
        P.op("scalar", _cp(B.MB[0:8, 6272:6400], gt[0:8, 0:128]),
             reads=[B.r_bank[7]], writes=[B.r_mb[12]])
        pb = bank(B, 2, 2)
        grpb = []
        for dh in range(2):
            P.op("tensor",
                 _mm(pb[:, dh * 512:(dh + 1) * 512], B.MB[0:8, 6272:6400],
                     B.B2SB[0:8, dh * 512:dh * 512 + 512], True, True),
                 reads=[B.r_mb[12], B.r_b2],
                 writes=[B.r_bank[2], B.r_bank[3]],
                 group=grpb, last=(dh == 1))
        P.op("vector", _tt(xtile(B, t), xtile(B, t), pb, ALU.add),
             reads=[B.r_bank[2], B.r_bank[3], B.r_x[t]], writes=[B.r_x[t]])
        kpb = B.MB[:, 6152:6160]
        P.op("vector", _tcopy(kpb, kp),
             reads=[B.r_scr, B.r_mb[12]], writes=[B.r_mb[12]])
        pos = B.PS[:, 6 * 512 + 8:6 * 512 + 16]
        pgrp = []
        P.op("tensor", _mm(pos, B.SUB[:], kpb, True, False),
             reads=[B.r_sub, B.r_mb[12], B.r_bank[6]],
             writes=[B.r_bank[6]], group=pgrp, last=False)
        P.op("tensor", _mm(pos, B.ONESR[:], base, False, True),
             reads=[B.r_onesf, r_base], writes=[B.r_bank[6]], group=pgrp,
             last=True)
        cntp = B.PS[0:1, 7 * 512:7 * 512 + 8]
        P.op("tensor", _mm(cntp, B.ONESF[:], kp, True, True),
             reads=[B.r_onesf, B.r_scr, B.r_bank[7]], writes=[B.r_bank[7]])
        P.op("vector", _ts(nk, kp, -BIGIX, BIGIX, ALU.mult, ALU.add),
             reads=[B.r_scr], writes=[B.r_scr])
        P.op("vector", _tt(posm, pos, nk, ALU.add),
             reads=[B.r_bank[6], B.r_scr], writes=[B.r_scr])
        for e in range(E):
            P.op("vector", _ts(B.IDXW[:, e:e + 1], posm[:, e:e + 1],
                               float(e * CAP), None, ALU.add),
                 reads=[B.r_scr, B.r_idxw], writes=[B.r_idxw])
        for e in range(E):
            P.op("gpsimd", _iscatter(
                B.IDXD, B.IDXW[:, e:e + 1], B.IOTA[:, t:t + 1],
                B.RB_IDX),
                reads=[B.r_idxw, B.r_iota, B.r_idxd], writes=[B.r_idxd],
                dma=True, sem="gdq")
        P.op("vector", _tcopy(cnt_s, cntp),
             reads=[B.r_bank[7], r_base], writes=[r_base])
        P.op("vector", _tt(base, base, cnt_s, ALU.add),
             reads=[r_base], writes=[r_base])

    cbp = B.PS[:, 7 * 512 + 8:7 * 512 + 16]
    P.op("tensor", _mm(cbp, B.ONESR[:], base, True, True),
         reads=[B.r_onesf, r_base, B.r_bank[7]], writes=[B.r_bank[7]])
    P.op("vector", _tcopy(B.CNTB[:], cbp),
         reads=[B.r_bank[7], B.r_cntb], writes=[B.r_cntb])
    for g in range(6):
        P.op("vector", _ts(B.SMSK[:, g * 8:(g + 1) * 8], B.CNTB[:],
                           B.IOTAF[:, g:g + 1], 0.0, ALU.subtract, ALU.is_gt),
             reads=[B.r_cntb, B.r_iota, B.r_smsk], writes=[B.r_smsk])
    for e in range(E):
        for g in range(6):
            P.op("gpsimd", _dma(
                B.IDXS[:, e * 6 + g:e * 6 + g + 1],
                B.IDXD[e * CAP + g * 128:e * CAP + (g + 1) * 128, :]),
                reads=[B.r_idxd, B.r_idxs], writes=[B.r_idxs], dma=True,
                sem="gdq")
    for e in range(E):
        for g in range(6):
            P.op("gpsimd", _igather(
                B.GG[:, g * 8 + e:g * 8 + e + 1], B.GD,
                B.IDXS[:, e * 6 + g:e * 6 + g + 1], B.RB_TOK, eoff=e),
                reads=[B.r_idxs, B.r_gd, B.r_gg], writes=[B.r_gg],
                dma=True, sem="gdq")
    for g in range(6):
        P.op("vector", _tt(B.GG[:, g * 8:(g + 1) * 8],
                           B.GG[:, g * 8:(g + 1) * 8],
                           B.SMSK[:, g * 8:(g + 1) * 8], ALU.mult),
             reads=[B.r_smsk, B.r_gg], writes=[B.r_gg])

    P.op("sync", _dma(XOUTB.rearrange("(t p) d -> p t d", p=128),
                      B.XBUF[:].rearrange("p (t d) -> p t d", d=D)),
         reads=list(B.r_x) + [B.r_xout], writes=[B.r_xout], dma=True,
         sem="sst")

    w1 = W["L1_ffn_w1"]
    w2 = W["L1_ffn_w2"]
    b1 = W["L1_ffn_b1"]
    widths = ((0, 512), (512, 256))
    for ei in range(E):
        P.op("gpsimd", _dma(B.MB[0:32, 6464:6592],
                            b1[ei].rearrange("(c p) -> c p", p=128)),
             reads=[B.r_mb[12]], writes=[B.r_mb[12]], dma=True, sem="gs3")
        b1t = bank(B, 7, dt=BF16)
        P.op("tensor", _tr(b1t[0:128, 0:32], B.MB[0:32, 6464:6592],
                           B.IDN[0:32, 0:32]),
             reads=[B.r_mb[12], B.r_idn], writes=[B.r_bank[7]])
        P.op("scalar", _cp(st[:, 416:448], b1t[0:128, 0:32]),
             reads=[B.r_bank[7], B.r_aux[4]], writes=[B.r_aux[4]])
        for g in range(6):
            P.op("gpsimd", _igather(
                B.HB[:, g * 1024:(g + 1) * 1024], B.XND,
                B.IDXS[:, ei * 6 + g:ei * 6 + g + 1], B.RB_TOK),
                reads=[B.r_idxs, B.r_xnd, B.r_hb[g // 2]],
                writes=[B.r_hb[g // 2]], dma=True, sem="gdq")
        for c in range(8):
            bks = [B.r_bank[c % 2]]
            pb = bank(B, c % 2, 1, dt=BF16)
            grp = []
            for g in range(6):
                P.op("tensor",
                     _tr(pb[:, g * 128:(g + 1) * 128],
                         B.HB[:, g * 1024 + c * 128:g * 1024 + c * 128 + 128],
                         B.IDN[:]),
                     reads=[B.r_hb[g // 2], B.r_idn],
                     writes=bks if g in (0, 5) else [],
                     group=grp, last=(g == 5))
            P.op("scalar", _cp(B.ABUF[:, c * S:c * S + CAP], pb[:, 0:CAP]),
                 reads=bks, writes=[B.r_xnT[c]])
        for half in range(2):
            w1s = w1[ei][:, half * 2048:(half + 1) * 2048].rearrange(
                "(c p) h -> p c h", p=128)
            w2s = w2[ei][half * 2048:(half + 1) * 2048, :].rearrange(
                "(c p) d -> p c d", p=128)
            cast_load(P, B, B.WB1[:].rearrange("p (c h) -> p c h", c=8),
                      w1s, B.r_wb1)
            cast_load(P, B,
                      B.WB2[:, 0:16384].rearrange("p (c d) -> p c d", c=16),
                      w2s, B.r_wb2)
            for off, wdt in widths:
                for hc in range(16):
                    bk = hc % 2
                    pb = bank(B, bk)[:, 0:wdt]
                    grp = []
                    for c in range(8):
                        lcol = c * 2048 + hc * 128
                        P.op("tensor",
                             _mm(pb, B.WB1[:, lcol:lcol + 128],
                                 B.ABUF[:, c * S + off:c * S + off + wdt],
                                 c == 0, c == 7),
                             reads=[B.r_wb1, B.r_xnT[c]],
                             writes=[B.r_bank[bk]] if c in (0, 7) else [],
                             group=grp, last=(c == 7))
                    bia = st[:, 416 + half * 16 + hc:417 + half * 16 + hc]
                    P.op("scalar", _act(B.HB[:, hc * 512:hc * 512 + wdt], pb,
                                        AF.Relu, bias=bia),
                         reads=[B.r_bank[bk], B.r_aux[4]],
                         writes=[B.r_hb[hc // 4]])
                for tt in range(wdt // 128):
                    g = off // 128 + tt
                    b0 = 2 + (tt % 2) * 2
                    pb = bank(B, b0, 2)
                    bks = [B.r_bank[b0], B.r_bank[b0 + 1]]
                    grp = []
                    for hc in range(16):
                        for dh in range(2):
                            lcol = hc * 512 + tt * 128
                            rcol = hc * 1024 + dh * 512
                            P.op("tensor",
                                 _mm(pb[:, dh * 512:(dh + 1) * 512],
                                     B.HB[:, lcol:lcol + 128],
                                     B.WB2[:, rcol:rcol + 512],
                                     hc == 0, hc == 15),
                                 reads=[B.r_hb[hc // 4], B.r_wb2],
                                 writes=bks
                                 if (hc, dh) in ((0, 0), (15, 1)) else [],
                                 group=grp, last=(hc == 15 and dh == 1))
                    stg = (g % 2)
                    ystage = B.MB[:, stg * 2048:(stg + 1) * 2048].bitcast(F32)
                    r_y = [B.r_mb[stg * 4 + j] for j in range(4)]
                    gcol = B.GG[:, g * 8 + ei:g * 8 + ei + 1]
                    P.op("vector", _ts(ystage, pb, gcol, None, ALU.mult),
                         reads=bks + [B.r_gg] + r_y, writes=r_y)
                    P.op("gpsimd", _iscatter(
                        XOUTB, B.IDXS[:, ei * 6 + g:ei * 6 + g + 1],
                        ystage, B.RB_TOK, ALU.add),
                        reads=r_y + [B.r_idxs, B.r_xout], writes=[B.r_xout],
                        dma=True, sem="sct")

WEIGHT_SPECS = [
    ("L0_attn_W_dq", [D, DC]), ("L0_attn_W_uq", [DC, D]),
    ("L0_attn_W_dkv", [D, DC]), ("L0_attn_W_uk", [DC, D]),
    ("L0_attn_W_uv", [DC, D]), ("L0_attn_W_o", [D, D]),
    ("L0_ffn_weight", [D, D]),
    ("L1_attn_Wq", [H, D, DK]), ("L1_attn_Wk", [H, D, DK]),
    ("L1_attn_Wv", [H, D, DK]),
    ("L1_ffn_router_w", [D, E]), ("L1_ffn_router_b", [E]),
    ("L1_ffn_w1", [E, D, HID]), ("L1_ffn_b1", [E, HID]),
    ("L1_ffn_w2", [E, HID, D]), ("L1_ffn_b2", [E, D]),
]

PHASE_EMIT = {
    "l0attn": emit_l0attn,
    "l0ffn": emit_l0ffn,
    "l1attn": emit_l1attn,
    "moe": emit_moe,
}


def build_program(phases, n_batches=NB):
    import contextlib
    nc = bass.Bass("TRN2", target_bir_lowering=False)
    Xin = nc.dram_tensor("x", [n_batches, S, D], F32, kind="ExternalInput").ap()
    Xout = nc.dram_tensor("y", [n_batches, S, D], F32,
                          kind="ExternalOutput").ap()
    W = {}
    for name, shape in WEIGHT_SPECS:
        W[name] = nc.dram_tensor(name, shape, F32, kind="ExternalInput").ap()

    ctx = contextlib.ExitStack()
    with ctx:
        B = alloc_bufs(nc, ctx)
        B.RB_TOK = ctx.enter_context(nc.gpsimd.register("rb_tok"))
        B.RB_IDX = ctx.enter_context(nc.gpsimd.register("rb_idx"))
        B.PD = [nc.dram_tensor(f"PD{i}", [S, S], BF16).ap()
                for i in range(H)]
        B.XND = nc.dram_tensor("XND", [S, D], BF16).ap()
        B.GD = nc.dram_tensor("GD", [S, E], F32).ap()
        B.IDXD = nc.dram_tensor("IDXD", [E * CAP, 1], I32).ap()
        B.AD = nc.dram_tensor("AD", [D, S], BF16).ap()
        B.r_pd = [Res(f"pd{i}") for i in range(H)]
        B.r_ad = Res("ad")
        sems = {}
        for sname in ["pe", "ac", "dv", "po", "spq", "sst", "gdq", "scq",
                      "gs1", "gs2", "gs3", "sct"]:
            sems[sname] = ctx.enter_context(nc.semaphore(sname))
        P = Prog(nc)
        emit_setup(P, B)
        for b in range(n_batches):
            emit_load_x(P, B, Xin, b)
            for ph in phases:
                if ph == "moes":
                    emit_ln(P, B, XND=B.XND)
                    emit_moes(P, B, W, Xout, b)
                else:
                    emit_ln(P, B)
                    PHASE_EMIT[ph](P, B, W)
            if "moes" not in phases:
                emit_store_x(P, B, Xout, b)
        P.barrier_wait("sync", [P.now("sst"), P.now("sct"), P.now("gdq")])
        with nc.Block() as block:
            P.replay(block, sems)
    return nc


def _in_maps(inputs, n_cores=8):
    x = np.ascontiguousarray(inputs["L0_attn_x"], dtype=np.float32)
    maps = []
    for c in range(n_cores):
        m = {"x": x[c * NB:(c + 1) * NB]}
        for name, _ in WEIGHT_SPECS:
            m[name] = np.ascontiguousarray(inputs[name], dtype=np.float32)
        maps.append(m)
    return maps


def kernel(**inputs):
    nc = build_program(["l0attn", "l0ffn", "l1attn", "moe"])
    maps = _in_maps(inputs)
    res = run_bass_kernel_spmd(nc, maps, core_ids=list(range(8)))
    return np.concatenate([r["y"] for r in res.results], axis=0)
